# version 14
# baseline (speedup 1.0000x reference)
import numpy as np
import contextlib
import concourse.bass as bass
import concourse.mybir as mybir
from concourse.bass_utils import run_bass_kernel_spmd

F32 = mybir.dt.float32
BF16 = mybir.dt.bfloat16
AX = mybir.AxisListType.X
AF = mybir.ActivationFunctionType
ALU = mybir.AluOpType

TOK, CDIM, HID = 2048, 2048, 5632
NT, NJ, NH = 16, 16, 16
MAGIC = 12582912.0

(S_SUM, S_SQ, S_MX, S_MN, S_M, S_VAR, S_R, S_MXC, S_SA, S_DQA, S_RSA,
 S_NMR, S_NMRSA, S_DQ3, S_MXH, S_SH) = range(16)


class Prog:
    ENGINES = ("sync", "vector", "scalar", "tensor", "gpsimd")

    def __init__(self):
        self.q = {e: [] for e in self.ENGINES}
        self.sems = {}
        self.cnt = {}

    def emit(self, eng, fn):
        self.q[eng].append(fn)

    def wait(self, eng, sname, val):
        if val and val > 0:
            self.emit(eng, lambda e, s=self.sems[sname], v=val: e.wait_ge(s, v))

    def do(self, eng, build, inc=None, n=1):
        if inc is None:
            self.emit(eng, build)
            return None
        self.cnt[inc] += n
        sem = self.sems[inc]

        def run(e, b=build, s=sem, k=n):
            b(e).then_inc(s, k)
        self.emit(eng, run)
        return self.cnt[inc]

    def c(self, name):
        return self.cnt[name]


def build_nc():
    nc = bass.Bass("TRN2", target_bir_lowering=False)

    x_d = nc.dram_tensor("x", [TOK, CDIM], F32, kind="ExternalInput").ap()
    wqt_d = nc.dram_tensor("wqt", [CDIM, CDIM], F32, kind="ExternalInput").ap()
    wkt_d = nc.dram_tensor("wkt", [CDIM, CDIM], F32, kind="ExternalInput").ap()
    wvt_d = nc.dram_tensor("wvt", [CDIM, CDIM], F32, kind="ExternalInput").ap()
    wot_d = nc.dram_tensor("wot", [CDIM, CDIM], F32, kind="ExternalInput").ap()
    w1t_d = nc.dram_tensor("w1t", [CDIM, HID], F32, kind="ExternalInput").ap()
    w2t_d = nc.dram_tensor("w2t", [CDIM, HID], F32, kind="ExternalInput").ap()
    w3t_d = nc.dram_tensor("w3t", [HID, CDIM], F32, kind="ExternalInput").ap()
    wsc_d = nc.dram_tensor("wsc", [128, 3 * 11264], F32, kind="ExternalInput").ap()
    idf_d = nc.dram_tensor("idf", [128, 128], F32, kind="ExternalInput").ap()
    out_d = nc.dram_tensor("out", [TOK, CDIM], F32, kind="ExternalOutput").ap()

    h_d = nc.dram_tensor("h_scr", [HID, TOK], BF16).ap()
    y2_d = nc.dram_tensor("y2_scr", [TOK, CDIM], BF16).ap()
    kvin_d = nc.dram_tensor("cc_kv_in", [128, 2064], F32).ap()
    kvout_d = nc.dram_tensor("cc_kv_out", [128, 2064], F32).ap()
    wsin_d = nc.dram_tensor("cc_ws_in", [128, 4], F32).ap()
    wsout_d = nc.dram_tensor("cc_ws_out", [128, 4], F32).ap()

    A0 = 16512
    B0 = A0 + 65536
    C0 = B0 + 65536

    def at(name, shape, dtype, off):
        h = nc.alloc_sbuf_tensor_at(name, shape, dtype, offset=off)
        per = int(np.prod(shape[1:])) * mybir.dt.size(dtype)
        assert off >= 16512 and off + per <= 229344, (name, off, off + per)
        return h

    xnT = at("xnT", [128, NJ, TOK], BF16, A0)
    ApreT = at("ApreT", [128, NJ, TOK], BF16, A0)
    out3 = at("out3", [128, 8, CDIM], F32, A0)
    mxacc = at("mxacc", [128, CDIM], F32, A0)
    h1t = at("h1t", [128, 512], F32, A0 + 8192)
    h2t = at("h2t", [128, 512], F32, A0 + 10240)
    wfst2 = at("wfst2", [128, 2, NJ, 128], F32, A0 + 16384)
    qtm2 = at("qtm2", [128, CDIM], F32, A0 + 32768)
    qT = at("qT", [128, NJ, TOK], BF16, B0)
    KTs = at("KTs", [128, NJ, TOK], BF16, B0)
    wkb = at("wkb", [128, 2, NJ, 512], BF16, B0)
    wvb = at("wvb", [128, 2, NJ, 512], BF16, B0 + 32768)
    t3b = at("t3b", [128, 2, 4, CDIM], BF16, B0)
    khb = at("khb", [128, 2, 4, 1024], BF16, B0 + 32768)
    hinb = at("hinb", [128, 2, 4, 1024], BF16, B0 + 49152)
    c0 = C0
    c1 = C0 + 16384
    c2 = C0 + 24576
    c3 = C0 + 32768
    c4 = C0 + 40960
    c5 = C0 + 49152
    c6 = C0 + 65536
    cP = C0 + 73728
    xtile = at("xtile", [128, 2, CDIM], F32, c0)
    wsst = at("wsst", [128, 2, 1024], F32, c1)
    wfst = at("wfst", [128, 2, NJ, 128], F32, c0)
    wf3 = at("wf3", [128, 4, CDIM], F32, c0)
    xsqd = at("xsqd", [128, CDIM], F32, c1)
    kvsb = at("kvsb", [128, NH, 128], F32, c1)
    kpre = at("kpre", [128, CDIM], F32, c1)
    wostg = at("wostg", [128, 2, 2, 512], F32, c3)
    xtl2 = at("xtl2", [128, CDIM], F32, c1)
    kvbf = at("kvbf", [128, NH, 129], BF16, c2)
    wstg = at("wstg", [128, 2, 2, 512], F32, c2)
    ktok = at("ktok", [128, 2, CDIM], BF16, c2)
    qtmp = at("qtmp", [128, CDIM], F32, c2)
    shr = at("shr", [1, CDIM], F32, c2)
    xnb = at("xnb", [128, 2, CDIM], BF16, c3)
    qt2 = at("qt2", [128, 2, CDIM], F32, c5)
    wob = at("wob", [128, NJ, 512], BF16, c5)
    outb = at("outb", [128, CDIM], F32, c3)
    kbuf = at("kbuf", [128, 2, 512], BF16, c4)
    vbuf = at("vbuf", [128, 2, 512], BF16, c4 + 2048)
    apre = at("apre", [128, 2, CDIM], BF16, c4)
    y2pc = at("y2pc", [128, 2, 512], BF16, c4)
    hrow = at("hrow", [128, 2, CDIM], BF16, c4)
    y2in = at("y2in", [128, 2, CDIM], BF16, c4)
    elut = at("elut", [128, 2, 512], F32, c5)
    elue = at("elue", [128, 2, 512], F32, c5 + 4096)
    wqb = at("wqb", [128, 2, NJ, 128], BF16, c5 + 8192)
    t1b = at("t1b", [128, 2, NJ, 128], BF16, c5)
    t2b = at("t2b", [128, 2, NJ, 128], BF16, c5 + 8192)
    bcsh = at("bcsh", [128, CDIM], F32, c6)
    stats = at("stats", [128, 256], F32, cP)
    swcols = at("swcols", [128, 16], F32, cP + 1024)
    swrow = at("swrow", [1, 16], F32, cP + 1088)
    ksb = at("ksb", [128, 16], F32, cP + 1152)
    onesb = at("onesb", [128, 1], BF16, cP + 1216)
    onesf_r = at("onesf_r", [1, 128], F32, cP + 1248)
    onesf_c = at("onesf_c", [128, 1], F32, cP + 1760)
    identf = at("identf", [128, 128], F32, cP + 1792)
    identb = at("identb", [128, 128], BF16, cP + 2304)
    zsb = at("zsb", [128, 2, 16], F32, cP + 2560)
    rows = at("rows", [16, 128], F32, cP + 2688)
    nmag = at("nmag", [128, 1], F32, cP + 3200)
    czero = at("czero", [128, 1], F32, cP + 3232)
    cone = at("cone", [128, 1], F32, cP + 3264)
    pmag = at("pmag", [128, 1], F32, cP + 3296)
    qh2 = at("qh2", [128, 1024], F32, cP + 3328)
    wsabs = at("wsabs", [128, 36], F32, cP + 7424)
    nc.const_aps.register(F32, 0.0, czero[:])
    nc.const_aps.register(F32, 1.0, cone[:])

    P = Prog()
    stack = contextlib.ExitStack()
    ps = stack.enter_context(nc.psum_tensor("ps", [128, 4096], F32))
    for s in ("sW", "sXa", "sXb", "sM", "sHa", "sHb", "sHl", "sYa", "sYb",
              "sYl", "sOa", "sOb", "cc1", "cc2", "vC", "aC", "tC", "sI",
              "sKv", "sHm", "sWS", "pC", "sW2", "sWSb", "sF2a", "sF2b"):
        P.sems[s] = stack.enter_context(nc.semaphore(s))
        P.cnt[s] = 0

    def st(slot, i=None):
        if i is None:
            return stats[:, slot * 16:(slot + 1) * 16]
        return stats[:, slot * 16 + i: slot * 16 + i + 1]

    def pbank(b, w=1):
        return ps[:, b * 512:(b + w) * 512]

    W, D = P.wait, P.do

    D("gpsimd", lambda e: e.dma_start(out=identf[:], in_=idf_d), inc="sI", n=16)
    for fn in (lambda e: e.memset(czero[:], 0.0),
               lambda e: e.memset(cone[:], 1.0),
               lambda e: e.memset(onesb[:], 1.0),
               lambda e: e.memset(onesf_r[:], 1.0),
               lambda e: e.memset(onesf_c[:], 1.0),
               lambda e: e.memset(swcols[:], 0.0),
               lambda e: e.memset(stats[:], 0.0),
               lambda e: e.memset(nmag[:], -MAGIC),
               lambda e: e.memset(pmag[:], MAGIC),
               lambda e: e.memset(wsabs[:], 0.0)):
        t = D("vector", fn, inc="vC")
        W("vector", "vC", t)
    W("vector", "sI", 16)
    v_identb = D("vector", lambda e: e.tensor_copy(identb[:], identf[:]), inc="vC")
    v_init = v_identb
    W("vector", "vC", v_init)

    x0_dma, p0_v, p0_a = {}, {}, {}
    for i in range(NT):
        if i >= 2:
            W("gpsimd", "vC", p0_v[i - 2])
            W("gpsimd", "aC", p0_a[i - 2])
        x0_dma[i] = D("gpsimd", lambda e, i=i: e.dma_start(
            out=xtile[:, i % 2], in_=x_d[i * 128:(i + 1) * 128, :]),
            inc="sXa" if i % 2 == 0 else "sXb", n=16)
        W("scalar", "sXa" if i % 2 == 0 else "sXb", x0_dma[i])
        if i == 0:
            W("scalar", "vC", v_init)
        if i >= 1:
            W("scalar", "aC", p0_a[i - 1])
        aid = D("scalar", lambda e, i=i: e.activation(
            xsqd[:], xtile[:, i % 2], AF.Identity, accum_out=st(S_SUM, i)),
            inc="aC")
        W("scalar", "aC", aid)
        p0_a[i] = D("scalar", lambda e, i=i: e.activation(
            xsqd[:], xtile[:, i % 2], AF.Square, accum_out=st(S_SQ, i)),
            inc="aC")
        W("vector", "sXa" if i % 2 == 0 else "sXb", x0_dma[i])
        D("vector", lambda e, i=i: e.reduce_max(
            st(S_MX, i), xtile[:, i % 2], axis=AX), inc="vC")
        p0_v[i] = D("vector", lambda e, i=i: e.tensor_reduce(
            st(S_MN, i), xtile[:, i % 2], axis=AX, op=ALU.min), inc="vC")

    W("vector", "aC", p0_a[NT - 1])
    W("vector", "vC", p0_v[NT - 1])

    def colop(build):
        t = D("vector", build, inc="vC")
        W("vector", "vC", t)
        return t

    colop(lambda e: e.tensor_scalar(st(S_M), st(S_SUM), 1.0 / CDIM, None, ALU.mult))
    colop(lambda e: e.tensor_scalar(st(S_VAR), st(S_SQ), 1.0 / CDIM, None, ALU.mult))
    colop(lambda e: e.tensor_tensor(st(S_SQ), st(S_M), st(S_M), ALU.mult))
    colop(lambda e: e.tensor_tensor(st(S_VAR), st(S_VAR), st(S_SQ), ALU.subtract))
    v_var = colop(lambda e: e.tensor_scalar(st(S_VAR), st(S_VAR), 1e-5, None,
                                            ALU.add))
    W("scalar", "aC", p0_a[NT - 1])
    W("scalar", "vC", v_var)
    a_sq = D("scalar", lambda e: e.activation(st(S_SQ), st(S_VAR), AF.Sqrt),
             inc="aC")
    W("vector", "aC", a_sq)
    colop(lambda e: e.reciprocal(st(S_R), st(S_SQ)))
    colop(lambda e: e.tensor_tensor(st(S_MX), st(S_MX), st(S_M), ALU.subtract))
    colop(lambda e: e.tensor_tensor(st(S_MN), st(S_M), st(S_MN), ALU.subtract))
    colop(lambda e: e.tensor_tensor(st(S_MX), st(S_MX), st(S_MN), ALU.max))
    colop(lambda e: e.tensor_tensor(st(S_MXC), st(S_MX), st(S_R), ALU.mult))
    colop(lambda e: e.tensor_scalar(st(S_MXC), st(S_MXC), 1e-5, None, ALU.max))
    colop(lambda e: e.reciprocal(st(S_SA), st(S_MXC)))
    colop(lambda e: e.tensor_scalar(st(S_SA), st(S_SA), 127.0, None, ALU.mult))
    colop(lambda e: e.tensor_scalar(st(S_DQA), st(S_MXC), 1.0 / 127.0, None,
                                    ALU.mult))
    colop(lambda e: e.tensor_tensor(st(S_RSA), st(S_R), st(S_SA), ALU.mult))
    colop(lambda e: e.scalar_tensor_tensor(st(S_NMR), st(S_M), -1.0, st(S_R),
                                           ALU.mult, ALU.mult))
    p0_cols = colop(lambda e: e.scalar_tensor_tensor(
        st(S_NMRSA), st(S_M), -1.0, st(S_RSA), ALU.mult, ALU.mult))

    trs = {"gidx": 0, "ev": {}}

    def transpose_tile(src_ap, dst, tcol):
        for g in range(4):
            gi = trs["gidx"] + g
            b = gi % 8
            if gi - 8 in trs["ev"]:
                W("tensor", "vC", trs["ev"][gi - 8])
            for jj in range(4):
                j = g * 4 + jj
                kw = dict(inc="tC") if jj == 3 else {}
                D("tensor", lambda e, j=j, b=b, jj=jj: e.matmul(
                    ps[:, b * 512 + jj * 128: b * 512 + jj * 128 + 128],
                    src_ap[:, j * 128:(j + 1) * 128], identb[:],
                    start=(jj == 0), stop=(jj == 3)), **kw)
            W("vector", "tC", P.c("tC"))
            trs["ev"][gi] = D("vector", lambda e, g=g, b=b: e.tensor_copy(
                dst[:, g * 4:(g + 1) * 4, tcol:tcol + 128],
                pbank(b).rearrange("p (a c) -> p a c", a=4)), inc="vC")
        trs["gidx"] += 4
        return P.c("tC"), trs["ev"][trs["gidx"] - 1]

    W("gpsimd", "vC", p0_v[NT - 1])
    W("gpsimd", "aC", p0_a[NT - 1])
    p1_xn, p1_tr = {}, {}
    for i in range(NT):
        if i >= 2:
            W("gpsimd", "aC", p1_xn[i - 2])
        d = D("gpsimd", lambda e, i=i: e.dma_start(
            out=xtile[:, i % 2], in_=x_d[i * 128:(i + 1) * 128, :]),
            inc="sXa" if i % 2 == 0 else "sXb", n=16)
        W("scalar", "sXa" if i % 2 == 0 else "sXb", d)
        if i == 0:
            W("scalar", "vC", p0_cols)
        if i >= 2:
            W("scalar", "tC", p1_tr[i - 2][0])
        p1_xn[i] = D("scalar", lambda e, i=i: e.activation(
            xnb[:, i % 2], xtile[:, i % 2], AF.Identity,
            bias=st(S_NMR, i), scale=st(S_R, i)), inc="aC")
        W("tensor", "aC", p1_xn[i])
        if i == 0:
            W("tensor", "vC", v_identb)
        p1_tr[i] = transpose_tile(xnb[:, i % 2], xnT, i * 128)
    p1_last_ev = p1_tr[NT - 1][1]

    kv_ev, p2_kev, p2_vev, kvmm = {}, {}, {}, {}
    prev_piece_cast = None
    piece_by_par = {}
    wcast = {}
    for oc in range(4):
        par = oc % 2
        for wi, (wt_d, dstb) in enumerate(((wkt_d, wkb), (wvt_d, wvb))):
            for q in range(8):
                gq2 = (oc * 2 + wi) * 8 + q
                wsem2 = "sW" if gq2 % 2 == 0 else "sW2"
                if piece_by_par.get(gq2 % 2) is not None:
                    W("sync", "vC", piece_by_par[gq2 % 2])
                D("sync", lambda e, wt_d=wt_d, oc=oc, q=q: e.dma_start(
                    out=wstg[:, q % 2],
                    in_=wt_d[:, oc * 512:(oc + 1) * 512].rearrange(
                        "(j p) o -> p j o", p=128)[:, q * 2:(q + 1) * 2, :]),
                    inc=wsem2, n=16)
                W("vector", wsem2, P.c(wsem2))
                if q == 0 and oc >= 2:
                    W("vector", "tC", kvmm[(oc - 2, NT - 1)])
                if q == 0 and oc == 0 and wi == 0:
                    W("vector", "vC", p1_tr[NT - 1][1])
                prev_piece_cast = D(
                    "vector", lambda e, dstb=dstb, par=par, q=q: e.tensor_copy(
                        dstb[:, par, q * 2:(q + 1) * 2, :], wstg[:, q % 2]),
                    inc="vC")
                piece_by_par[gq2 % 2] = prev_piece_cast
            wcast[(oc, wi)] = prev_piece_cast
        wv_cast = wcast[(oc, 1)]
        for t in range(NT):
            kb_, vb_ = t % 2, 2 + t % 2
            tpar = t % 2
            if t == 0:
                W("tensor", "vC", wv_cast)
            if oc == 0 and t == 0:
                W("tensor", "vC", p1_last_ev)
            if t >= 2:
                W("tensor", "vC", p2_kev[(oc, t - 2)])
                W("tensor", "aC", p2_vev[(oc, t - 2)])
            elif oc >= 1:
                W("tensor", "vC", p2_kev[(oc - 1, NT - 2 + t)])
                W("tensor", "aC", p2_vev[(oc - 1, NT - 2 + t)])
            if t == 0 and oc >= 2:
                W("tensor", "vC", kv_ev[oc - 2])
            for j in range(NJ):
                D("tensor", lambda e, j=j, t=t, kb_=kb_, par=par: e.matmul(
                    pbank(kb_), xnT[:, j, t * 128:(t + 1) * 128],
                    wkb[:, par, j, :],
                    start=(j == 0), stop=(j == NJ - 1)),
                    **(dict(inc="tC") if j == NJ - 1 else {}))
            kproj = P.c("tC")
            for j in range(NJ):
                D("tensor", lambda e, j=j, t=t, vb_=vb_, par=par: e.matmul(
                    pbank(vb_), xnT[:, j, t * 128:(t + 1) * 128],
                    wvb[:, par, j, :],
                    start=(j == 0), stop=(j == NJ - 1)),
                    **(dict(inc="tC") if j == NJ - 1 else {}))
            vproj = P.c("tC")
            W("vector", "tC", kproj)
            e1 = D("vector", lambda e, kb_=kb_, tpar=tpar: e.tensor_scalar(
                elut[:, tpar], pbank(kb_), 0.0, None, ALU.min), inc="vC")
            W("scalar", "vC", e1)
            e2 = D("scalar", lambda e, tpar=tpar: e.activation(
                elue[:, tpar], elut[:, tpar], AF.Exp), inc="aC")
            W("vector", "aC", e2)
            p2_kev[(oc, t)] = D(
                "vector", lambda e, kb_=kb_, tpar=tpar: e.scalar_tensor_tensor(
                    kbuf[:, tpar], pbank(kb_), 0.0, elue[:, tpar],
                    ALU.max, ALU.add), inc="vC")
            W("scalar", "tC", vproj)
            p2_vev[(oc, t)] = D(
                "scalar", lambda e, vb_=vb_, tpar=tpar: e.activation(
                    vbuf[:, tpar], pbank(vb_), AF.Identity), inc="aC")

            def kv_group(oc, t):
                tpar = t % 2
                W("tensor", "vC", p2_kev[(oc, t)])
                W("tensor", "aC", p2_vev[(oc, t)])
                kvb = 4 if oc % 2 == 0 else 6
                for hh in range(4):
                    D("tensor", lambda e, hh=hh, t=t, kvb=kvb, tpar=tpar:
                      e.matmul(
                          ps[:, kvb * 512 + hh * 128:
                             kvb * 512 + (hh + 1) * 128],
                          kbuf[:, tpar, hh * 128:(hh + 1) * 128],
                          vbuf[:, tpar, hh * 128:(hh + 1) * 128],
                          start=(t == 0 and hh == 0),
                          stop=(t == NT - 1 and hh == 3)))
                    h = oc * 4 + hh
                    D("tensor", lambda e, h=h, t=t, tpar=tpar, hh=hh, oc=oc:
                      e.matmul(
                          ps[:, 5 * 512 + h: 5 * 512 + h + 1],
                          kbuf[:, tpar, hh * 128:(hh + 1) * 128], onesb[:],
                          start=(oc == 0 and t == 0 and hh == 0),
                          stop=(oc == 3 and t == NT - 1 and hh == 3)),
                          **(dict(inc="tC") if hh == 3 else {}))
                kvmm[(oc, t)] = P.c("tC")
            if t >= 1:
                kv_group(oc, t - 1)
            if t == NT - 1:
                kv_group(oc, t)
        W("vector", "tC", kvmm[(oc, NT - 1)])
        kv_ev[oc] = D("vector", lambda e, kvb=4 if oc % 2 == 0 else 6, oc=oc:
                      e.tensor_copy(kvsb[:, oc * 4:(oc + 1) * 4, :],
                                    pbank(kvb).rearrange("p (a c) -> p a c", a=4)),
                      inc="vC")
    W("vector", "tC", kvmm[(3, NT - 1)])
    ks_ev = D("vector", lambda e: e.tensor_copy(
        ksb[:], ps[:, 5 * 512: 5 * 512 + 16]), inc="vC")

    W("gpsimd", "vC", ks_ev)
    W("gpsimd", "vC", kv_ev[3])
    D("gpsimd", lambda e: e.dma_start(
        out=kvin_d[:, 0:2048], in_=kvsb[:].rearrange("p a c -> p (a c)")),
        inc="sKv", n=16)
    g2 = D("gpsimd", lambda e: e.dma_start(
        out=kvin_d[:, 2048:2064], in_=ksb[:]), inc="sKv", n=16)
    W("gpsimd", "sKv", g2)
    cc1_done = D("gpsimd", lambda e: e.collective_compute(
        "AllReduce", ALU.add, replica_groups=[[0, 1], [2, 3], [4, 5], [6, 7]],
        ins=[kvin_d.opt()], outs=[kvout_d.opt()]), inc="cc1")
    W("gpsimd", "cc1", cc1_done)
    D("gpsimd", lambda e: e.dma_start(
        out=kvsb[:].rearrange("p a c -> p (a c)"), in_=kvout_d[:, 0:2048]),
        inc="sKv", n=16)
    g3 = D("gpsimd", lambda e: e.dma_start(
        out=ksb[:], in_=kvout_d[:, 2048:2064]), inc="sKv", n=16)
    W("vector", "sKv", g3)
    kvc1 = D("vector", lambda e: e.tensor_copy(
        kvbf[:, :, 0:128], kvsb[:]), inc="vC")
    kvbf_c = D("vector", lambda e: e.tensor_copy(
        kvbf[:, :, 128:129], ksb[:].rearrange("p (a c) -> p a c", c=1)),
        inc="vC")

    ws_abs = {}
    for i in range(33):
        wsem = "sWS" if i % 2 == 0 else "sWSb"
        if i >= 2:
            W("gpsimd", "aC", ws_abs[i - 2])
        elif i == 0:
            W("gpsimd", "vC", kvbf_c)
        d = D("gpsimd", lambda e, i=i: e.dma_start(
            out=wsst[:, i % 2], in_=wsc_d[:, i * 1024:(i + 1) * 1024]),
            inc=wsem, n=16)
        W("scalar", wsem, d)
        ws_abs[i] = D("scalar", lambda e, i=i: e.activation(
            xsqd[:, 0:1024] if False else wsst[:, i % 2], wsst[:, i % 2],
            AF.Abs, accum_out=wsabs[:, i:i + 1]), inc="aC")
    W("vector", "aC", ws_abs[32])
    ws_sum = {}
    for w in range(3):
        t = D("vector", lambda e, w=w: e.reduce_sum(
            swcols[:, w:w + 1], wsabs[:, w * 11:(w + 1) * 11], axis=AX),
            inc="vC")
        W("vector", "vC", t)
        ws_sum[w] = t
    ws_done = ws_sum[2]
    W("gpsimd", "vC", ws_done)
    g1 = D("gpsimd", lambda e: e.dma_start(out=wsin_d, in_=swcols[:, 0:4]),
           inc="sM", n=16)
    W("gpsimd", "sM", g1)
    cc2_done = D("gpsimd", lambda e: e.collective_compute(
        "AllReduce", ALU.add, replica_groups=[list(range(8))],
        ins=[wsin_d.opt()], outs=[wsout_d.opt()]), inc="cc2")
    W("gpsimd", "cc2", cc2_done)
    ws_back = D("gpsimd", lambda e: e.dma_start(out=swcols[:, 0:4], in_=wsout_d),
                inc="sM", n=16)

    q_ev, wq_cast, qmm = {}, {}, {}
    for ot in range(NJ):
        wsemq = "sW" if ot % 2 == 0 else "sW2"
        if ot == 0:
            W("sync", "aC", p1_xn[NT - 1])
            W("sync", "vC", wv_cast)
        if ot >= 2:
            W("sync", "vC", wq_cast[ot - 2])
        for half in range(2):
            D("sync", lambda e, ot=ot, half=half: e.dma_start(
                out=wfst[:, ot % 2, half * 8:(half + 1) * 8, :],
                in_=wqt_d[:, ot * 128:(ot + 1) * 128].rearrange(
                    "(j p) o -> p j o", p=128)[:, half * 8:(half + 1) * 8, :]),
                inc=wsemq, n=16)
        W("vector", wsemq, P.c(wsemq))
        if ot >= 2:
            W("vector", "tC", qmm[ot - 2])
        wq_cast[ot] = D("vector", lambda e, ot=ot: e.tensor_copy(
            wqb[:, ot % 2], wfst[:, ot % 2]), inc="vC")
        for tc in range(4):
            gi = ot * 4 + tc
            b = gi % 8
            par = gi % 2
            if tc == 0:
                W("tensor", "vC", wq_cast[ot])
            if gi >= 8:
                W("tensor", "vC", q_ev[gi - 8])
            for j in range(NJ):
                D("tensor", lambda e, j=j, ot=ot, tc=tc, b=b: e.matmul(
                    pbank(b), wqb[:, ot % 2, j, :],
                    xnT[:, j, tc * 512:(tc + 1) * 512],
                    start=(j == 0), stop=(j == NJ - 1)),
                    **(dict(inc="tC") if j == NJ - 1 else {}))
            qmm[ot] = P.c("tC")
            W("vector", "tC", qmm[ot])
            e1 = D("vector", lambda e, b=b, par=par: e.tensor_scalar(
                elut[:, par], pbank(b), 0.0, None, ALU.min), inc="vC")
            W("scalar", "vC", e1)
            e2 = D("scalar", lambda e, par=par: e.activation(
                elue[:, par], elut[:, par], AF.Exp), inc="aC")
            W("vector", "aC", e2)
            q_ev[gi] = D("vector", lambda e, b=b, par=par, ot=ot, tc=tc:
                         e.scalar_tensor_tensor(
                             qT[:, ot, tc * 512:(tc + 1) * 512], pbank(b), 0.0,
                             elue[:, par], ALU.max, ALU.add), inc="vC")
    p3_last = q_ev[63]

    W("tensor", "vC", kvbf_c)
    W("tensor", "vC", p3_last)
    p4_z, p4_tev, p4_trmm = {}, {}, {}
    for tc in range(NT):
        par = tc % 2
        ob0 = 0 if par == 0 else 4
        dbk = 4 if par == 0 else 0
        if tc >= 2:
            W("tensor", "vC", p4_tev[tc - 2][-1])
        if tc >= 1:
            W("tensor", "vC", p4_tev[tc - 1][0])
            W("tensor", "vC", p4_z[tc - 1])
        for h in range(NH):
            D("tensor", lambda e, h=h, tc=tc, dbk=dbk: e.matmul(
                ps[:, dbk * 512 + h: dbk * 512 + h + 1],
                qT[:, h, tc * 128:(tc + 1) * 128], kvbf[:, h, 128:129],
                start=(h == 0), stop=(h == NH - 1)),
                **(dict(inc="tC") if h == NH - 1 else {}))
        den_mm = P.c("tC")
        for h in range(NH):
            b = ob0 + h // 4
            D("tensor", lambda e, h=h, tc=tc, b=b: e.matmul(
                ps[:, b * 512 + (h % 4) * 128: b * 512 + (h % 4) * 128 + 128],
                qT[:, h, tc * 128:(tc + 1) * 128], kvbf[:, h, 0:128],
                start=(h % 4 == 0), stop=(h % 4 == 3)),
                **(dict(inc="tC") if h == NH - 1 else {}))
        out_mm = P.c("tC")
        W("vector", "tC", den_mm)
        z1 = D("vector", lambda e, dbk=dbk, par=par: e.tensor_scalar(
            zsb[:, par], ps[:, dbk * 512: dbk * 512 + 16], 1e-6, None, ALU.add),
            inc="vC")
        W("vector", "vC", z1)
        p4_z[tc] = D("vector", lambda e, par=par: e.reciprocal(
            zsb[:, par], zsb[:, par]), inc="vC")
        W("vector", "tC", out_mm)
        W("vector", "vC", p4_z[tc])
        if tc == 0:
            W("vector", "tC", kvmm[(3, NT - 1)])
        if tc >= 2:
            W("vector", "tC", p4_trmm[tc - 2])
        W("scalar", "tC", out_mm)
        W("scalar", "vC", p4_z[tc])
        if tc == 0:
            W("scalar", "tC", kvmm[(3, NT - 1)])
        if tc >= 2:
            W("scalar", "tC", p4_trmm[tc - 2])
        for h in range(8):
            b = ob0 + h // 4
            kw = dict(inc="aC") if h == 7 else {}
            D("scalar", lambda e, h=h, b=b, par=par: e.activation(
                apre[:, par, h * 128:(h + 1) * 128],
                ps[:, b * 512 + (h % 4) * 128: b * 512 + (h % 4) * 128 + 128],
                AF.Identity, scale=zsb[:, par, h:h + 1]), **kw)
        aev_a = P.c("aC")
        for h in range(8, NH):
            b = ob0 + h // 4
            kw = dict(inc="vC") if h == NH - 1 else {}
            D("vector", lambda e, h=h, b=b, par=par: e.tensor_scalar(
                apre[:, par, h * 128:(h + 1) * 128],
                ps[:, b * 512 + (h % 4) * 128: b * 512 + (h % 4) * 128 + 128],
                zsb[:, par, h:h + 1], None, ALU.mult), **kw)
        aev = P.c("vC")
        W("tensor", "vC", aev)
        W("tensor", "aC", aev_a)
        evs = []
        for g in range(4):
            b = ob0 + g
            for jj in range(4):
                j = g * 4 + jj
                kw = dict(inc="tC") if jj == 3 else {}
                D("tensor", lambda e, j=j, b=b, jj=jj, par=par: e.matmul(
                    ps[:, b * 512 + jj * 128: b * 512 + jj * 128 + 128],
                    apre[:, par, j * 128:(j + 1) * 128], identb[:],
                    start=(jj == 0), stop=(jj == 3)), **kw)
            W("vector", "tC", P.c("tC"))
            evs.append(D("vector", lambda e, g=g, b=b, tc=tc: e.tensor_copy(
                ApreT[:, g * 4:(g + 1) * 4, tc * 128:(tc + 1) * 128],
                pbank(b).rearrange("p (a c) -> p a c", a=4)), inc="vC"))
        p4_trmm[tc] = P.c("tC")
        p4_tev[tc] = evs
    p4_last_ev = p4_tev[NT - 1][-1]
    p4_prev_ev = p4_tev[NT - 2][-1]
    p4_end_mm = P.c("tC")

    W("tensor", "vC", p4_last_ev)
    W("tensor", "vC", p4_prev_ev)
    W("gpsimd", "vC", wq_cast[NJ - 1])
    W("gpsimd", "aC", p1_xn[NT - 1])
    p1b_ka, p1b_k2, p1b_tr = {}, {}, {}
    trsb = {"gidx": 0, "ev": {}}

    def p1b_transpose_tile(src_ap, tcol):
        for g in range(4):
            gi = trsb["gidx"] + g
            b = gi % 4
            if gi - 4 in trsb["ev"]:
                W("tensor", "vC", trsb["ev"][gi - 4])
            for jj in range(4):
                j = g * 4 + jj
                kw = dict(inc="tC") if jj == 3 else {}
                D("tensor", lambda e, j=j, b=b, jj=jj: e.matmul(
                    ps[:, b * 512 + jj * 128: b * 512 + jj * 128 + 128],
                    src_ap[:, j * 128:(j + 1) * 128], identb[:],
                    start=(jj == 0), stop=(jj == 3)), **kw)
            W("vector", "tC", P.c("tC"))
            trsb["ev"][gi] = D("vector", lambda e, g=g, b=b, tcol=tcol:
                               e.tensor_copy(
                                   KTs[:, g * 4:(g + 1) * 4, tcol:tcol + 128],
                                   pbank(b).rearrange("p (a c) -> p a c", a=4)),
                               inc="vC")
        trsb["gidx"] += 4
        return P.c("tC"), trsb["ev"][trsb["gidx"] - 1]

    def p1b_tile(i):
        if i >= 2:
            W("gpsimd", "aC", p1b_ka[i - 2])
        d = D("gpsimd", lambda e, i=i: e.dma_start(
            out=xtile[:, i % 2], in_=x_d[i * 128:(i + 1) * 128, :]),
            inc="sXa" if i % 2 == 0 else "sXb", n=16)
        W("scalar", "sXa" if i % 2 == 0 else "sXb", d)
        if i == 0:
            W("scalar", "aC", ws_abs[32])
        if i >= 1:
            W("scalar", "vC", p1b_k2[i - 1])
        p1b_ka[i] = D("scalar", lambda e, i=i: e.activation(
            kpre[:], xtile[:, i % 2], AF.Identity,
            bias=st(S_NMRSA, i), scale=st(S_RSA, i)), inc="aC")
        W("vector", "aC", p1b_ka[i])
        if i == 0:
            W("vector", "tC", p4_end_mm)
        if i >= 2:
            W("vector", "tC", p1b_tr[i - 2][0])
        k1 = D("vector", lambda e, i=i: e.tensor_scalar(
            kpre[:], kpre[:], MAGIC, MAGIC, ALU.add, ALU.subtract), inc="vC")
        W("vector", "vC", k1)
        p1b_k2[i] = D("vector", lambda e, i=i: e.tensor_scalar(
            ktok[:, i % 2], kpre[:], st(S_DQA, i), None, ALU.mult), inc="vC")
        W("tensor", "vC", p1b_k2[i])
        p1b_tr[i] = p1b_transpose_tile(ktok[:, i % 2], i * 128)

    wo_cast, p5_ev, p5_st, p5_mm, p5_q = {}, {}, {}, {}, {}
    NOC = 4

    def p5_stage(oc):
        for q in range(8):
            gq = oc * 8 + q
            wsemo = "sW" if gq % 2 == 0 else "sW2"
            if gq == 0:
                W("sync", "vC", wq_cast[NJ - 1])
                W("sync", "vC", wq_cast[NJ - 2])
            if gq >= 2:
                W("sync", "aC", p5_q[gq - 2])
            D("sync", lambda e, oc=oc, q=q: e.dma_start(
                out=wostg[:, q % 2],
                in_=wot_d[:, oc * 512:(oc + 1) * 512].rearrange(
                    "(j p) o -> p j o", p=128)[:, q * 2:(q + 1) * 2, :]),
                inc=wsemo, n=16)
            dW = P.c(wsemo)
            W("scalar", wsemo, dW)
            if gq == 0:
                W("scalar", "tC", qmm[NJ - 1])
                W("scalar", "vC", p3_last)
            if q == 0 and oc >= 1:
                W("scalar", "tC", p5_mm[oc - 1])
            p5_q[gq] = D("scalar", lambda e, q=q: e.activation(
                wob[:, q * 2:(q + 1) * 2, :].rearrange("p a b -> p (a b)"),
                wostg[:, q % 2].rearrange("p a b -> p (a b)"), AF.Identity),
                inc="aC")
        wo_cast[oc] = p5_q[oc * 8 + 7]

    def p5_group(oc, tc):
        gi = oc * NT + tc
        b = 4 + gi % 4
        if tc == 0:
            W("tensor", "aC", wo_cast[oc])
        if gi >= 4:
            W("tensor", "aC", p5_ev[gi - 4])
        for j in range(NJ):
            D("tensor", lambda e, j=j, tc=tc, b=b: e.matmul(
                pbank(b), ApreT[:, j, tc * 128:(tc + 1) * 128],
                wob[:, j, :], start=(j == 0), stop=(j == NJ - 1)),
                **(dict(inc="tC") if j == NJ - 1 else {}))
        p5_mm[oc] = P.c("tC")
        W("scalar", "tC", p5_mm[oc])
        if gi >= 2:
            W("scalar", "sYa" if gi % 2 == 0 else "sYb", p5_st[gi - 2])
        p5_ev[gi] = D("scalar", lambda e, b=b, par=gi % 2: e.activation(
            y2pc[:, par], pbank(b), AF.Identity), inc="aC")
        W("gpsimd", "aC", p5_ev[gi])
        p5_st[gi] = D("gpsimd", lambda e, tc=tc, oc=oc, par=gi % 2:
                      e.dma_start(
                          out=y2_d[tc * 128:(tc + 1) * 128,
                                   oc * 512:(oc + 1) * 512],
                          in_=y2pc[:, par]), inc="sYa" if gi % 2 == 0
                      else "sYb", n=16)

    for u in range(16):
        if u % 4 == 0:
            p5_stage(u // 4)
        p1b_tile(u)
        for tcq in range(4):
            p5_group(u // 4, (u % 4) * 4 + tcq)
    p1b_last = p1b_tr[NT - 1][1]
    p5_final_mm = P.c("tC")
    p5_final_aev = P.c("aC")

    W("tensor", "sM", ws_back)
    W("tensor", "aC", p5_final_aev)
    D("tensor", lambda e: e.matmul(
        ps[0:1, 7 * 512: 7 * 512 + 4], onesf_c[:], swcols[:, 0:4],
        start=True, stop=True), inc="tC")
    t_ws = P.c("tC")
    W("scalar", "tC", t_ws)
    a_ws = D("scalar", lambda e: e.activation(
        swrow[:, 0:4], ps[0:1, 7 * 512: 7 * 512 + 4], AF.Identity), inc="aC")
    W("vector", "aC", a_ws)
    r1 = D("vector", lambda e: e.tensor_scalar(
        swrow[:, 4:8], swrow[:, 0:4], 1.0 / (HID * CDIM), 1e-5,
        ALU.mult, ALU.max), inc="vC")
    W("vector", "vC", r1)
    r2 = D("vector", lambda e: e.reciprocal(swrow[:, 8:12], swrow[:, 4:8]),
           inc="vC")
    W("vector", "vC", r2)
    W("tensor", "vC", r2)
    for w in range(3):
        D("tensor", lambda e, w=w: e.matmul(
            ps[:, 7 * 512 + 16 + w: 7 * 512 + 16 + w + 1],
            onesf_r[:], swrow[:, 8 + w: 8 + w + 1], start=(w == 0),
            stop=False))
        D("tensor", lambda e, w=w: e.matmul(
            ps[:, 7 * 512 + 24 + w: 7 * 512 + 24 + w + 1],
            onesf_r[:], swrow[:, 4 + w: 4 + w + 1], start=False,
            stop=(w == 2)),
            **(dict(inc="tC") if w == 2 else {}))
    t_bc = P.c("tC")
    W("vector", "tC", t_bc)
    sw_c1 = D("vector", lambda e: e.tensor_copy(
        swcols[:, 4:7], ps[:, 7 * 512 + 16: 7 * 512 + 19]), inc="vC")
    sw_c2 = D("vector", lambda e: e.tensor_copy(
        swcols[:, 9:12], ps[:, 7 * 512 + 24: 7 * 512 + 27]), inc="vC")

    W("vector", "tC", p5_final_mm)
    mx_init = D("vector", lambda e: e.memset(mxacc[:], 0.0), inc="vC")

    fa_t1, fa_t2, fa_hd, fa_mm, hst, fa_q2 = {}, {}, {}, {}, {}, {}
    fa_q1 = {}

    def emit_fa_quant(m):
        wsem = "sW" if m % 2 == 0 else "sW2"
        if m == 0:
            W("sync", "aC", p1b_ka[NT - 1])
            W("sync", "aC", p5_q[NOC * 8 - 1])
        if m >= 2:
            W("sync", "vC", fa_q1[m - 2])
        for half in range(2):
            D("sync", lambda e, m=m, half=half: e.dma_start(
                out=wfst[:, m % 2, half * 8:(half + 1) * 8, :],
                in_=w1t_d[:, m * 128:(m + 1) * 128].rearrange(
                    "(j p) h -> p j h", p=128)[:, half * 8:(half + 1) * 8, :]),
                inc=wsem, n=16)
        d1 = P.c(wsem)
        W("vector", wsem, d1)
        if m == 0:
            W("vector", "vC", sw_c2)
            W("vector", "tC", p1b_tr[NT - 1][0])
        if m >= 2:
            W("vector", "tC", fa_mm[m - 2])
        wa = wfst[:, m % 2].rearrange("p a b -> p (a b)")
        q1 = D("vector", lambda e, wa=wa: e.tensor_scalar(
            qtmp[:], wa, swcols[:, 4:5], MAGIC, ALU.mult, ALU.add), inc="vC")
        fa_q1[m] = q1
        W("vector", "vC", q1)
        c1v = D("vector", lambda e: e.tensor_scalar(
            qtmp[:], qtmp[:], MAGIC - 1.0, MAGIC + 1.0, ALU.max, ALU.min),
            inc="vC")
        W("vector", "vC", c1v)
        fa_t1[m] = D("vector", lambda e, m=m: e.tensor_scalar(
            t1b[:, m % 2].rearrange("p a b -> p (a b)"), qtmp[:],
            MAGIC, None, ALU.subtract), inc="vC")
        if m == 0:
            W("sync", "tC", p5_final_mm)
        wsem2c = "sF2a" if m % 2 == 0 else "sF2b"
        if m >= 2:
            W("sync", "vC", fa_q2[m - 2])
        for half in range(2):
            D("sync", lambda e, m=m, half=half: e.dma_start(
                out=wfst2[:, m % 2, half * 8:(half + 1) * 8, :],
                in_=w2t_d[:, m * 128:(m + 1) * 128].rearrange(
                    "(j p) h -> p j h", p=128)[:, half * 8:(half + 1) * 8, :]),
                inc=wsem2c, n=16)
        d2 = P.c(wsem2c)
        W("vector", wsem2c, d2)
        wa2 = wfst2[:, m % 2].rearrange("p a b -> p (a b)")
        if m == 0:
            W("vector", "tC", p5_final_mm)
        q2 = D("vector", lambda e, wa2=wa2: e.tensor_scalar(
            qtm2[:], wa2, swcols[:, 5:6], MAGIC, ALU.mult, ALU.add), inc="vC")
        fa_q2[m] = q2
        W("vector", "vC", q2)
        c2v = D("vector", lambda e: e.tensor_scalar(
            qtm2[:], qtm2[:], MAGIC - 1.0, MAGIC + 1.0, ALU.max, ALU.min),
            inc="vC")
        W("vector", "vC", c2v)
        fa_t2[m] = D("vector", lambda e, m=m: e.tensor_scalar(
            t2b[:, m % 2].rearrange("p a b -> p (a b)"), qtm2[:],
            MAGIC, None, ALU.subtract), inc="vC")

    def emit_fa_pe(m):
        for tc in range(4):
            it = m * 4 + tc
            by1, by2 = it % 4, 4 + it % 4
            if tc == 0:
                W("tensor", "vC", fa_t2[m])
            if it == 0:
                W("tensor", "vC", mx_init)
            if it >= 4:
                W("tensor", "aC", fa_hd[it - 4][0])
                W("tensor", "vC", fa_hd[it - 4][1])
            for j in range(NJ):
                D("tensor", lambda e, j=j, m=m, tc=tc, by1=by1: e.matmul(
                    pbank(by1), t1b[:, m % 2, j, :],
                    KTs[:, j, tc * 512:(tc + 1) * 512],
                    start=(j == 0), stop=(j == NJ - 1)),
                    **(dict(inc="tC") if j == NJ - 1 else {}))
            mm1 = P.c("tC")
            for j in range(NJ):
                D("tensor", lambda e, j=j, m=m, tc=tc, by2=by2: e.matmul(
                    pbank(by2), t2b[:, m % 2, j, :],
                    KTs[:, j, tc * 512:(tc + 1) * 512],
                    start=(j == 0), stop=(j == NJ - 1)),
                    **(dict(inc="tC") if j == NJ - 1 else {}))
            mm2 = P.c("tC")
            fa_mm[m] = mm2
            W("scalar", "tC", mm1)
            if it >= 1:
                W("scalar", "vC", fa_hd[it - 1][1])
            s1 = D("scalar", lambda e, by1=by1: e.activation(
                h1t[:], pbank(by1), AF.Silu, scale=swcols[:, 9:10]), inc="aC")
            W("scalar", "tC", mm2)
            s2 = D("scalar", lambda e, by2=by2: e.activation(
                h2t[:], pbank(by2), AF.Identity, scale=swcols[:, 10:11]),
                inc="aC")
            W("vector", "aC", s2)
            if m >= 2:
                W("vector", "sHa" if m % 2 == 0 else "sHb", hst[m - 2])
            if it == 0:
                W("vector", "sYa", p5_st[62])
                W("vector", "sYb", p5_st[63])
            hv = D("vector", lambda e, m=m, tc=tc: e.tensor_tensor(
                hrow[:, m % 2, tc * 512:(tc + 1) * 512], h1t[:], h2t[:],
                ALU.mult), inc="vC")
            W("scalar", "vC", hv)
            habs = D("scalar", lambda e, m=m, tc=tc: e.activation(
                h1t[:], hrow[:, m % 2, tc * 512:(tc + 1) * 512], AF.Abs),
                inc="aC")
            W("vector", "aC", habs)
            mxv = D("vector", lambda e, tc=tc: e.tensor_tensor(
                mxacc[:, tc * 512:(tc + 1) * 512],
                mxacc[:, tc * 512:(tc + 1) * 512], h1t[:], ALU.max),
                inc="vC")
            fa_hd[it] = (s2, mxv, hv)
        W("gpsimd", "vC", fa_hd[m * 4 + 3][1])
        hst[m] = D("gpsimd", lambda e, m=m: e.dma_start(
            out=h_d[m * 128:(m + 1) * 128, :], in_=hrow[:, m % 2]),
            inc="sHa" if m % 2 == 0 else "sHb", n=16)

    emit_fa_quant(0)
    for m in range(44):
        if m + 1 < 44:
            emit_fa_quant(m + 1)
        emit_fa_pe(m)
    fa_last_mm = fa_mm[43]
    fa_last_aev = P.c("aC")

    W("tensor", "vC", fa_hd[44 * 4 - 1][1])
    W("tensor", "aC", fa_last_aev)
    fold_ev = {}
    for bi in range(16):
        b = bi % 4
        if bi >= 4:
            W("tensor", "vC", fold_ev[bi - 4])
        D("tensor", lambda e, bi=bi, b=b: e.matmul(
            ps[:, b * 512: b * 512 + 128],
            mxacc[:, bi * 128:(bi + 1) * 128], identf[:],
            start=True, stop=True), inc="tC")
        W("vector", "tC", P.c("tC"))
        fold_ev[bi] = D("vector", lambda e, bi=bi, b=b: e.reduce_max(
            st(S_MXH, bi), ps[:, b * 512: b * 512 + 128], axis=AX), inc="vC")
    W("vector", "vC", fold_ev[15])
    c1_ = D("vector", lambda e: e.tensor_scalar(
        st(S_MXH), st(S_MXH), 1e-5, None, ALU.max), inc="vC")
    W("vector", "vC", c1_)
    d3 = D("vector", lambda e: e.tensor_scalar(
        st(S_DQ3), st(S_MXH), 1.0 / 127.0, swcols[:, 11:12],
        ALU.mult, ALU.mult), inc="vC")
    shc = D("vector", lambda e: e.reciprocal(st(S_SH), st(S_MXH)), inc="vC")
    W("vector", "vC", shc)
    shc2 = D("vector", lambda e: e.tensor_scalar(
        st(S_SH), st(S_SH), 127.0, None, ALU.mult), inc="vC")
    W("tensor", "vC", shc2)
    D("tensor", lambda e: e.matmul(
        ps[0:16, 4 * 512: 4 * 512 + 128], st(S_SH), identf[:],
        start=True, stop=True), inc="tC")
    t_sh = P.c("tC")
    W("scalar", "tC", t_sh)
    a_shr = D("scalar", lambda e: e.activation(
        rows[:], ps[0:16, 4 * 512: 4 * 512 + 128], AF.Identity), inc="aC")
    W("gpsimd", "aC", a_shr)
    W("gpsimd", "tC", p1b_tr[NT - 1][0])
    W("gpsimd", "vC", d3)
    g_shr = D("gpsimd", lambda e: e.dma_start(
        out=shr[:], in_=rows[:]), inc="sM", n=16)
    W("tensor", "sM", g_shr)
    bcsh_ev = {}
    for q in range(4):
        bq = 5 + q % 2
        if q >= 2:
            W("tensor", "vC", bcsh_ev[q - 2])
        D("tensor", lambda e, q=q, bq=bq: e.matmul(
            pbank(bq), onesf_r[:], shr[:, q * 512:(q + 1) * 512],
            start=True, stop=True), inc="tC")
        W("vector", "tC", P.c("tC"))
        bcsh_ev[q] = D("vector", lambda e, q=q, bq=bq: e.tensor_copy(
            bcsh[:, q * 512:(q + 1) * 512], pbank(bq)), inc="vC")
    bcsh_done = bcsh_ev[3]

    NMAC_N = 11
    out_st = {}
    fstate = {"f1_prev": None, "fin2": {}, "o3r": {}}

    def emit_final_tile(g2f, tc):
        ti = g2f * 8 + tc
        lg = (g2f + 1) * NMAC_N - 1
        if g2f == 0 and tc == 0:
            W("gpsimd", "aC", wo_cast[NOC - 1])
            W("gpsimd", "sYa", p5_st[62])
            W("gpsimd", "sYb", p5_st[63])
        if (g2f, tc) != (0, 0):
            pst = out_st[(g2f, tc - 1)] if tc >= 1 else out_st[(0, 7)]
            W("gpsimd", "sOa" if (tc - 1) % 2 == 0 else "sOb", pst)
        dx = D("gpsimd", lambda e, ti=ti: e.dma_start(
            out=outb[:], in_=x_d[ti * 128:(ti + 1) * 128, :]),
            inc="sXa", n=16)
        dy = D("gpsimd", lambda e, ti=ti, tc=tc: e.dma_start(
            out=y2in[:, tc % 2], in_=y2_d[ti * 128:(ti + 1) * 128, :]),
            inc="sYl", n=16)
        sm, vv = fb_ev[(lg, tc)]
        W("vector", sm, vv)
        W("vector", "sXa", dx)
        f1 = D("vector", lambda e, tc=tc, ti=ti: e.scalar_tensor_tensor(
            outb[:], out3[:, tc, :], st(S_DQ3, ti),
            outb[:], ALU.mult, ALU.add), inc="vC")
        fstate.setdefault("f1", {})[(g2f, tc)] = f1
        W("gpsimd", "vC", f1)
        W("gpsimd", "sYl", dy)
        fin2 = D("gpsimd", lambda e, tc=tc: e.tensor_tensor(
            outb[:], outb[:], y2in[:, tc % 2], ALU.add), inc="pC")
        W("gpsimd", "pC", fin2)
        out_st[(g2f, tc)] = D("gpsimd", lambda e, ti=ti: e.dma_start(
            out=out_d[ti * 128:(ti + 1) * 128, :], in_=outb[:]),
            inc="sOa" if tc % 2 == 0 else "sOb", n=16)

    prev_mm_by_par = {0: P.c("tC"), 1: P.c("tC")}
    prev_kh_by_par = {0: None, 1: None}
    q3_par = {}
    qbox = {"prev_qh": None, "first_dh": True}
    fb_ev = {}
    fb_kh_all, fb_t3_all = {}, {}
    NSUB = 4

    def emit_quant(gmi):
        g2q = gmi // NMAC_N
        miq = gmi % NMAC_N
        tcoq = g2q * 1024
        par = gmi % 2
        h0 = miq * 512
        for s in range(NSUB):
            wsem3 = ("sW", "sW2", "sF2a", "sF2b")[s]
            if q3_par.get(s) is not None:
                W("sync", "aC", q3_par[s])
            if gmi == 0 and s == 0:
                W("sync", "vC", fa_t2[43])
            if gmi == 0 and s == 3:
                W("sync", "vC", bcsh_done)
            d = D("sync", lambda e, h0=h0, s=s: e.dma_start(
                out=wf3[:, s],
                in_=w3t_d[h0 + s * 128: h0 + (s + 1) * 128, :]),
                inc=wsem3, n=16)
            W("scalar", wsem3, d)
            if gmi == 0 and s == 0:
                W("scalar", "tC", fa_last_mm)
            if qbox.get("t3p", {}).get(s % 2) is not None:
                W("scalar", "vC", qbox["t3p"][s % 2])
            q3 = D("scalar", lambda e, s=s: e.activation(
                qt2[:, s % 2], wf3[:, s], AF.Identity,
                scale=swcols[:, 6:7], bias=pmag[:]), inc="aC")
            q3_par[s] = q3
            W("vector", "aC", q3)
            if s == 0:
                W("vector", "tC", prev_mm_by_par[par])
            c3v = D("vector", lambda e, s=s: e.tensor_scalar(
                qt2[:, s % 2], qt2[:, s % 2], MAGIC - 1.0, MAGIC + 1.0,
                ALU.max, ALU.min), inc="vC")
            W("vector", "vC", c3v)
            fb_t3_all[(gmi, s)] = ("vC", D(
                "vector", lambda e, s=s, par=par: e.tensor_scalar(
                    t3b[:, par, s, :], qt2[:, s % 2], MAGIC, None,
                    ALU.subtract), inc="vC"))
            qbox.setdefault("t3p", {})[s % 2] = fb_t3_all[(gmi, s)][1]
            if qbox["first_dh"]:
                W("gpsimd", "tC", fa_last_mm)
                W("gpsimd", "sHa", hst[42])
                W("gpsimd", "sHb", hst[43])
                qbox["first_dh"] = False
            if prev_kh_by_par[par] is not None:
                W("gpsimd", "vC", prev_kh_by_par[par])
            hsem = "sHl" if par == 0 else "sHm"
            dh = D("gpsimd", lambda e, h0=h0, s=s, tcoq=tcoq, par=par:
                   e.dma_start(
                       out=hinb[:, par, s, :],
                       in_=h_d[h0 + s * 128: h0 + (s + 1) * 128,
                               tcoq:tcoq + 1024]), inc=hsem, n=16)
            W("gpsimd", hsem, dh)
            W("gpsimd", "vC", bcsh_done)
            if qbox["prev_qh"] is not None:
                W("gpsimd", "vC", qbox["prev_qh"])
            qh = D("gpsimd", lambda e, s=s, tcoq=tcoq, par=par:
                   e.tensor_tensor(
                       qh2[:], hinb[:, par, s, :],
                       bcsh[:, tcoq:tcoq + 1024], ALU.mult), inc="pC")
            W("vector", "pC", qh)
            fb_kh_all[(gmi, s)] = D(
                "vector", lambda e, s=s, par=par: e.tensor_scalar(
                    khb[:, par, s, :], qh2[:], MAGIC, MAGIC,
                    ALU.add, ALU.subtract), inc="vC")
            qbox["prev_qh"] = fb_kh_all[(gmi, s)]
        prev_kh_by_par[par] = fb_kh_all[(gmi, NSUB - 1)]

    def emit_pe(gmi):
        miq = gmi % NMAC_N
        par = gmi % 2
        for tc in range(8):
            gb0 = 0 if tc % 2 == 0 else 4
            if tc == 0:
                W("tensor", "vC", fb_kh_all[(gmi, NSUB - 1)])
                W("tensor", fb_t3_all[(gmi, NSUB - 1)][0],
                  fb_t3_all[(gmi, NSUB - 1)][1])
            if tc >= 2:
                sm, vv = fb_ev[(gmi, tc - 2)]
                W("tensor", sm, vv)
            elif gmi >= 1:
                sm, vv = fb_ev[(gmi - 1, 6 + tc)]
                W("tensor", sm, vv)
            else:
                if tc == 0:
                    W("tensor", "vC", fold_ev[15])
                    W("tensor", "vC", bcsh_done)
                    W("tensor", "aC", a_shr)
            for s in range(NSUB):
                for oq in range(4):
                    kw = (dict(inc="tC")
                          if (s == NSUB - 1 and oq == 3) else {})
                    D("tensor", lambda e, s=s, oq=oq, tc=tc, gb0=gb0,
                      par=par:
                      e.matmul(
                          pbank(gb0 + oq),
                          khb[:, par, s, tc * 128:(tc + 1) * 128],
                          t3b[:, par, s, oq * 512:(oq + 1) * 512],
                          start=(s == 0), stop=(s == NSUB - 1)), **kw)
            fb_mm = P.c("tC")
            if tc == 7:
                prev_mm_by_par[par] = fb_mm
            if gmi == NMAC_N:
                emit_final_tile(0, tc)
            if miq == 0:
                W("scalar", "tC", fb_mm)
                if gmi == NMAC_N:
                    W("scalar", "vC", fstate["f1"][(0, tc)])
                v = D("scalar", lambda e, tc=tc, gb0=gb0: e.activation(
                    out3[:, tc, :], pbank(gb0, 4), AF.Identity),
                    inc="aC")
                fb_ev[(gmi, tc)] = ("aC", v)
            else:
                W("vector", "tC", fb_mm)
                v = D("vector", lambda e, tc=tc, gb0=gb0:
                      e.scalar_tensor_tensor(
                          out3[:, tc, :], pbank(gb0, 4), 1.0,
                          out3[:, tc, :], ALU.mult, ALU.add), inc="vC")
                fb_ev[(gmi, tc)] = ("vC", v)

    emit_quant(0)
    for gmi in range(2 * NMAC_N):
        if gmi + 1 < 2 * NMAC_N:
            emit_quant(gmi + 1)
        emit_pe(gmi)
    for tc in range(8):
        emit_final_tile(1, tc)
    W("gpsimd", "sOa", P.c("sOa"))
    W("gpsimd", "sOb", P.c("sOb"))

    with nc.Block() as block:
        @block.sync
        def _(sync):
            for fn in P.q["sync"]:
                fn(sync)

        @block.vector
        def _(vector):
            for fn in P.q["vector"]:
                fn(vector)

        @block.scalar
        def _(scalar):
            for fn in P.q["scalar"]:
                fn(scalar)

        @block.tensor
        def _(tensor):
            for fn in P.q["tensor"]:
                fn(tensor)

        @block.gpsimd
        def _(gpsimd):
            for fn in P.q["gpsimd"]:
                fn(gpsimd)

    return nc


_NC_CACHE = {}


def kernel(**inputs):
    x = np.asarray(inputs["L0_attn_x"], dtype=np.float32)
    Wq = np.asarray(inputs["L0_attn_Wq"], dtype=np.float32)
    Wk = np.asarray(inputs["L0_attn_Wk"], dtype=np.float32)
    Wv = np.asarray(inputs["L0_attn_Wv"], dtype=np.float32)
    Wo = np.asarray(inputs["L0_attn_Wo"], dtype=np.float32)
    w1 = np.asarray(inputs["L0_ffn_w1"], dtype=np.float32)
    w2 = np.asarray(inputs["L0_ffn_w2"], dtype=np.float32)
    w3 = np.asarray(inputs["L0_ffn_w3"], dtype=np.float32)

    wqt = np.ascontiguousarray(Wq.T)
    wkt = np.ascontiguousarray(Wk.T)
    wvt = np.ascontiguousarray(Wv.T)
    wot = np.ascontiguousarray(Wo.T)
    w1t = np.ascontiguousarray(w1.T)
    w2t = np.ascontiguousarray(w2.T)
    w3t = np.ascontiguousarray(w3.T)
    idf = np.eye(128, dtype=np.float32)

    nsl = HID * CDIM // 8
    in_maps = []
    for c in range(8):
        b, g = c // 2, c % 2
        xs = np.ascontiguousarray(x[b, g * TOK:(g + 1) * TOK, :])
        wsc = np.concatenate(
            [w.reshape(-1)[c * nsl:(c + 1) * nsl].reshape(128, nsl // 128)
             for w in (w1, w2, w3)], axis=1)
        in_maps.append({"x": xs, "wqt": wqt, "wkt": wkt, "wvt": wvt,
                        "wot": wot, "w1t": w1t, "w2t": w2t, "w3t": w3t,
                        "wsc": np.ascontiguousarray(wsc), "idf": idf})

    if "nc" not in _NC_CACHE:
        _NC_CACHE["nc"] = build_nc()
    res = run_bass_kernel_spmd(_NC_CACHE["nc"], in_maps, core_ids=list(range(8)))
    out = np.zeros((4, 4096, CDIM), dtype=np.float32)
    for c in range(8):
        b, g = c // 2, c % 2
        out[b, g * TOK:(g + 1) * TOK, :] = res.results[c]["out"]
    return out


# version 15
# speedup vs baseline: 1.0296x; 1.0296x over previous
import numpy as np
import contextlib
import concourse.bass as bass
import concourse.mybir as mybir
from concourse.bass_utils import run_bass_kernel_spmd

F32 = mybir.dt.float32
BF16 = mybir.dt.bfloat16
AX = mybir.AxisListType.X
AF = mybir.ActivationFunctionType
ALU = mybir.AluOpType

TOK, CDIM, HID = 2048, 2048, 5632
NT, NJ, NH = 16, 16, 16
MAGIC = 12582912.0

(S_SUM, S_SQ, S_MX, S_MN, S_M, S_VAR, S_R, S_MXC, S_SA, S_DQA, S_RSA,
 S_NMR, S_NMRSA, S_DQ3, S_MXH, S_SH) = range(16)


class Prog:
    ENGINES = ("sync", "vector", "scalar", "tensor", "gpsimd")

    def __init__(self):
        self.q = {e: [] for e in self.ENGINES}
        self.sems = {}
        self.cnt = {}

    def emit(self, eng, fn):
        self.q[eng].append(fn)

    def wait(self, eng, sname, val):
        if val and val > 0:
            self.emit(eng, lambda e, s=self.sems[sname], v=val: e.wait_ge(s, v))

    def do(self, eng, build, inc=None, n=1):
        if inc is None:
            self.emit(eng, build)
            return None
        self.cnt[inc] += n
        sem = self.sems[inc]

        def run(e, b=build, s=sem, k=n):
            b(e).then_inc(s, k)
        self.emit(eng, run)
        return self.cnt[inc]

    def c(self, name):
        return self.cnt[name]


def build_nc():
    nc = bass.Bass("TRN2", target_bir_lowering=False)

    x_d = nc.dram_tensor("x", [TOK, CDIM], F32, kind="ExternalInput").ap()
    wqt_d = nc.dram_tensor("wqt", [CDIM, CDIM], F32, kind="ExternalInput").ap()
    wkt_d = nc.dram_tensor("wkt", [CDIM, CDIM], F32, kind="ExternalInput").ap()
    wvt_d = nc.dram_tensor("wvt", [CDIM, CDIM], F32, kind="ExternalInput").ap()
    wot_d = nc.dram_tensor("wot", [CDIM, CDIM], F32, kind="ExternalInput").ap()
    w1t_d = nc.dram_tensor("w1t", [CDIM, HID], F32, kind="ExternalInput").ap()
    w2t_d = nc.dram_tensor("w2t", [CDIM, HID], F32, kind="ExternalInput").ap()
    w3t_d = nc.dram_tensor("w3t", [HID, CDIM], F32, kind="ExternalInput").ap()
    wsc_d = nc.dram_tensor("wsc", [128, 3 * 11264], F32, kind="ExternalInput").ap()
    idf_d = nc.dram_tensor("idf", [128, 128], F32, kind="ExternalInput").ap()
    out_d = nc.dram_tensor("out", [TOK, CDIM], F32, kind="ExternalOutput").ap()

    h_d = nc.dram_tensor("h_scr", [HID, TOK], BF16).ap()
    y2_d = nc.dram_tensor("y2_scr", [TOK, CDIM], BF16).ap()
    kvin_d = nc.dram_tensor("cc_kv_in", [128, 2064], F32).ap()
    kvout_d = nc.dram_tensor("cc_kv_out", [128, 2064], F32).ap()
    wsin_d = nc.dram_tensor("cc_ws_in", [128, 4], F32).ap()
    wsout_d = nc.dram_tensor("cc_ws_out", [128, 4], F32).ap()

    A0 = 16512
    B0 = A0 + 65536
    C0 = B0 + 65536

    def at(name, shape, dtype, off):
        h = nc.alloc_sbuf_tensor_at(name, shape, dtype, offset=off)
        per = int(np.prod(shape[1:])) * mybir.dt.size(dtype)
        assert off >= 16512 and off + per <= 229344, (name, off, off + per)
        return h

    xnT = at("xnT", [128, NJ, TOK], BF16, A0)
    ApreT = at("ApreT", [128, NJ, TOK], BF16, A0)
    out3 = at("out3", [128, 8, CDIM], F32, A0)
    mxacc = at("mxacc", [128, CDIM], F32, A0)
    h1t = at("h1t", [128, 512], F32, A0 + 8192)
    h2t = at("h2t", [128, 512], F32, A0 + 10240)
    wfst2 = at("wfst2", [128, 2, NJ, 128], F32, A0 + 16384)
    qtm2 = at("qtm2", [128, CDIM], F32, A0 + 32768)
    qT = at("qT", [128, NJ, TOK], BF16, B0)
    KTs = at("KTs", [128, NJ, TOK], BF16, B0)
    wkb = at("wkb", [128, 2, NJ, 512], BF16, B0)
    wvb = at("wvb", [128, 2, NJ, 512], BF16, B0 + 32768)
    t3b = at("t3b", [128, 2, 4, CDIM], BF16, B0)
    khb = at("khb", [128, 2, 4, 1024], BF16, B0 + 32768)
    hinb = at("hinb", [128, 4, 1024], BF16, B0 + 49152)
    qh2b = at("qh2b", [128, 2, 1024], F32, B0 + 57344)
    c0 = C0
    c1 = C0 + 16384
    c2 = C0 + 24576
    c3 = C0 + 32768
    c4 = C0 + 40960
    c5 = C0 + 49152
    c6 = C0 + 65536
    cP = C0 + 73728
    xtile = at("xtile", [128, 2, CDIM], F32, c0)
    wsst = at("wsst", [128, 2, 1024], F32, c1)
    wfst = at("wfst", [128, 2, NJ, 128], F32, c0)
    wf3 = at("wf3", [128, 4, CDIM], F32, c0)
    xsqd = at("xsqd", [128, CDIM], F32, c1)
    kvsb = at("kvsb", [128, NH, 128], F32, c1)
    kpre = at("kpre", [128, CDIM], F32, c1)
    wostg = at("wostg", [128, 2, 2, 512], F32, c3)
    xtl2 = at("xtl2", [128, CDIM], F32, c1)
    kvbf = at("kvbf", [128, NH, 129], BF16, c2)
    wstg = at("wstg", [128, 2, 2, 512], F32, c2)
    ktok = at("ktok", [128, 2, CDIM], BF16, c2)
    qtmp = at("qtmp", [128, CDIM], F32, c2)
    shr = at("shr", [1, CDIM], F32, c2)
    xnb = at("xnb", [128, 2, CDIM], BF16, c3)
    qt2 = at("qt2", [128, 2, CDIM], F32, c5)
    wob = at("wob", [128, NJ, 512], BF16, c5)
    outb = at("outb", [128, CDIM], F32, c3)
    kbuf = at("kbuf", [128, 2, 512], BF16, c4)
    vbuf = at("vbuf", [128, 2, 512], BF16, c4 + 2048)
    apre = at("apre", [128, 2, CDIM], BF16, c4)
    y2pc = at("y2pc", [128, 2, 512], BF16, c4)
    hrow = at("hrow", [128, 2, CDIM], BF16, c4)
    y2in = at("y2in", [128, 2, CDIM], BF16, c4)
    elut = at("elut", [128, 2, 512], F32, c5)
    elue = at("elue", [128, 2, 512], F32, c5 + 4096)
    wqb = at("wqb", [128, 2, NJ, 128], BF16, c5 + 8192)
    t1b = at("t1b", [128, 2, NJ, 128], BF16, c5)
    t2b = at("t2b", [128, 2, NJ, 128], BF16, c5 + 8192)
    bcsh = at("bcsh", [128, CDIM], F32, c6)
    stats = at("stats", [128, 256], F32, cP)
    swcols = at("swcols", [128, 16], F32, cP + 1024)
    swrow = at("swrow", [1, 16], F32, cP + 1088)
    ksb = at("ksb", [128, 16], F32, cP + 1152)
    onesb = at("onesb", [128, 1], BF16, cP + 1216)
    onesf_r = at("onesf_r", [1, 128], F32, cP + 1248)
    onesf_c = at("onesf_c", [128, 1], F32, cP + 1760)
    identf = at("identf", [128, 128], F32, cP + 1792)
    identb = at("identb", [128, 128], BF16, cP + 2304)
    zsb = at("zsb", [128, 2, 16], F32, cP + 2560)
    rows = at("rows", [16, 128], F32, cP + 2688)
    nmag = at("nmag", [128, 1], F32, cP + 3200)
    czero = at("czero", [128, 1], F32, cP + 3232)
    cone = at("cone", [128, 1], F32, cP + 3264)
    pmag = at("pmag", [128, 1], F32, cP + 3296)
    qh2 = at("qh2", [128, 1024], F32, cP + 3328)
    wsabs = at("wsabs", [128, 36], F32, cP + 7424)
    nc.const_aps.register(F32, 0.0, czero[:])
    nc.const_aps.register(F32, 1.0, cone[:])

    P = Prog()
    stack = contextlib.ExitStack()
    ps = stack.enter_context(nc.psum_tensor("ps", [128, 4096], F32))
    for s in ("sW", "sXa", "sXb", "sM", "sHa", "sHb", "sHl", "sYa", "sYb",
              "sYl", "sOa", "sOb", "cc1", "cc2", "vC", "aC", "tC", "sI",
              "sKv", "sHm", "sWS", "pC", "sW2", "sWSb", "sF2a", "sF2b"):
        P.sems[s] = stack.enter_context(nc.semaphore(s))
        P.cnt[s] = 0

    def st(slot, i=None):
        if i is None:
            return stats[:, slot * 16:(slot + 1) * 16]
        return stats[:, slot * 16 + i: slot * 16 + i + 1]

    def pbank(b, w=1):
        return ps[:, b * 512:(b + w) * 512]

    W, D = P.wait, P.do

    D("gpsimd", lambda e: e.dma_start(out=identf[:], in_=idf_d), inc="sI", n=16)
    for fn in (lambda e: e.memset(czero[:], 0.0),
               lambda e: e.memset(cone[:], 1.0),
               lambda e: e.memset(onesb[:], 1.0),
               lambda e: e.memset(onesf_r[:], 1.0),
               lambda e: e.memset(onesf_c[:], 1.0),
               lambda e: e.memset(swcols[:], 0.0),
               lambda e: e.memset(stats[:], 0.0),
               lambda e: e.memset(nmag[:], -MAGIC),
               lambda e: e.memset(pmag[:], MAGIC),
               lambda e: e.memset(wsabs[:], 0.0)):
        t = D("vector", fn, inc="vC")
        W("vector", "vC", t)
    W("vector", "sI", 16)
    v_identb = D("vector", lambda e: e.tensor_copy(identb[:], identf[:]), inc="vC")
    v_init = v_identb
    W("vector", "vC", v_init)

    x0_dma, p0_v, p0_a = {}, {}, {}
    for i in range(NT):
        if i >= 2:
            W("gpsimd", "vC", p0_v[i - 2])
            W("gpsimd", "aC", p0_a[i - 2])
        x0_dma[i] = D("gpsimd", lambda e, i=i: e.dma_start(
            out=xtile[:, i % 2], in_=x_d[i * 128:(i + 1) * 128, :]),
            inc="sXa" if i % 2 == 0 else "sXb", n=16)
        W("scalar", "sXa" if i % 2 == 0 else "sXb", x0_dma[i])
        if i == 0:
            W("scalar", "vC", v_init)
        if i >= 1:
            W("scalar", "aC", p0_a[i - 1])
        aid = D("scalar", lambda e, i=i: e.activation(
            xsqd[:], xtile[:, i % 2], AF.Identity, accum_out=st(S_SUM, i)),
            inc="aC")
        W("scalar", "aC", aid)
        p0_a[i] = D("scalar", lambda e, i=i: e.activation(
            xsqd[:], xtile[:, i % 2], AF.Square, accum_out=st(S_SQ, i)),
            inc="aC")
        W("vector", "sXa" if i % 2 == 0 else "sXb", x0_dma[i])
        D("vector", lambda e, i=i: e.reduce_max(
            st(S_MX, i), xtile[:, i % 2], axis=AX), inc="vC")
        p0_v[i] = D("vector", lambda e, i=i: e.tensor_reduce(
            st(S_MN, i), xtile[:, i % 2], axis=AX, op=ALU.min), inc="vC")

    W("vector", "aC", p0_a[NT - 1])
    W("vector", "vC", p0_v[NT - 1])

    def colop(build):
        t = D("vector", build, inc="vC")
        W("vector", "vC", t)
        return t

    colop(lambda e: e.tensor_scalar(st(S_M), st(S_SUM), 1.0 / CDIM, None, ALU.mult))
    colop(lambda e: e.tensor_scalar(st(S_VAR), st(S_SQ), 1.0 / CDIM, None, ALU.mult))
    colop(lambda e: e.tensor_tensor(st(S_SQ), st(S_M), st(S_M), ALU.mult))
    colop(lambda e: e.tensor_tensor(st(S_VAR), st(S_VAR), st(S_SQ), ALU.subtract))
    v_var = colop(lambda e: e.tensor_scalar(st(S_VAR), st(S_VAR), 1e-5, None,
                                            ALU.add))
    W("scalar", "aC", p0_a[NT - 1])
    W("scalar", "vC", v_var)
    a_sq = D("scalar", lambda e: e.activation(st(S_SQ), st(S_VAR), AF.Sqrt),
             inc="aC")
    W("vector", "aC", a_sq)
    colop(lambda e: e.reciprocal(st(S_R), st(S_SQ)))
    colop(lambda e: e.tensor_tensor(st(S_MX), st(S_MX), st(S_M), ALU.subtract))
    colop(lambda e: e.tensor_tensor(st(S_MN), st(S_M), st(S_MN), ALU.subtract))
    colop(lambda e: e.tensor_tensor(st(S_MX), st(S_MX), st(S_MN), ALU.max))
    colop(lambda e: e.tensor_tensor(st(S_MXC), st(S_MX), st(S_R), ALU.mult))
    colop(lambda e: e.tensor_scalar(st(S_MXC), st(S_MXC), 1e-5, None, ALU.max))
    colop(lambda e: e.reciprocal(st(S_SA), st(S_MXC)))
    colop(lambda e: e.tensor_scalar(st(S_SA), st(S_SA), 127.0, None, ALU.mult))
    colop(lambda e: e.tensor_scalar(st(S_DQA), st(S_MXC), 1.0 / 127.0, None,
                                    ALU.mult))
    colop(lambda e: e.tensor_tensor(st(S_RSA), st(S_R), st(S_SA), ALU.mult))
    colop(lambda e: e.scalar_tensor_tensor(st(S_NMR), st(S_M), -1.0, st(S_R),
                                           ALU.mult, ALU.mult))
    p0_cols = colop(lambda e: e.scalar_tensor_tensor(
        st(S_NMRSA), st(S_M), -1.0, st(S_RSA), ALU.mult, ALU.mult))

    trs = {"gidx": 0, "ev": {}}

    def transpose_tile(src_ap, dst, tcol):
        for g in range(4):
            gi = trs["gidx"] + g
            b = gi % 8
            if gi - 8 in trs["ev"]:
                W("tensor", "vC", trs["ev"][gi - 8])
            for jj in range(4):
                j = g * 4 + jj
                kw = dict(inc="tC") if jj == 3 else {}
                D("tensor", lambda e, j=j, b=b, jj=jj: e.matmul(
                    ps[:, b * 512 + jj * 128: b * 512 + jj * 128 + 128],
                    src_ap[:, j * 128:(j + 1) * 128], identb[:],
                    start=(jj == 0), stop=(jj == 3)), **kw)
            W("vector", "tC", P.c("tC"))
            trs["ev"][gi] = D("vector", lambda e, g=g, b=b: e.tensor_copy(
                dst[:, g * 4:(g + 1) * 4, tcol:tcol + 128],
                pbank(b).rearrange("p (a c) -> p a c", a=4)), inc="vC")
        trs["gidx"] += 4
        return P.c("tC"), trs["ev"][trs["gidx"] - 1]

    W("gpsimd", "vC", p0_v[NT - 1])
    W("gpsimd", "aC", p0_a[NT - 1])
    p1_xn, p1_tr = {}, {}
    for i in range(NT):
        if i >= 2:
            W("gpsimd", "aC", p1_xn[i - 2])
        d = D("gpsimd", lambda e, i=i: e.dma_start(
            out=xtile[:, i % 2], in_=x_d[i * 128:(i + 1) * 128, :]),
            inc="sXa" if i % 2 == 0 else "sXb", n=16)
        W("scalar", "sXa" if i % 2 == 0 else "sXb", d)
        if i == 0:
            W("scalar", "vC", p0_cols)
        if i >= 2:
            W("scalar", "tC", p1_tr[i - 2][0])
        p1_xn[i] = D("scalar", lambda e, i=i: e.activation(
            xnb[:, i % 2], xtile[:, i % 2], AF.Identity,
            bias=st(S_NMR, i), scale=st(S_R, i)), inc="aC")
        W("tensor", "aC", p1_xn[i])
        if i == 0:
            W("tensor", "vC", v_identb)
        p1_tr[i] = transpose_tile(xnb[:, i % 2], xnT, i * 128)
    p1_last_ev = p1_tr[NT - 1][1]

    kv_ev, p2_kev, p2_vev, kvmm = {}, {}, {}, {}
    prev_piece_cast = None
    piece_by_par = {}
    wcast = {}
    for oc in range(4):
        par = oc % 2
        for wi, (wt_d, dstb) in enumerate(((wkt_d, wkb), (wvt_d, wvb))):
            for q in range(8):
                gq2 = (oc * 2 + wi) * 8 + q
                wsem2 = "sW" if gq2 % 2 == 0 else "sW2"
                if piece_by_par.get(gq2 % 2) is not None:
                    W("sync", "vC", piece_by_par[gq2 % 2])
                D("sync", lambda e, wt_d=wt_d, oc=oc, q=q: e.dma_start(
                    out=wstg[:, q % 2],
                    in_=wt_d[:, oc * 512:(oc + 1) * 512].rearrange(
                        "(j p) o -> p j o", p=128)[:, q * 2:(q + 1) * 2, :]),
                    inc=wsem2, n=16)
                W("vector", wsem2, P.c(wsem2))
                if q == 0 and oc >= 2:
                    W("vector", "tC", kvmm[(oc - 2, NT - 1)])
                if q == 0 and oc == 0 and wi == 0:
                    W("vector", "vC", p1_tr[NT - 1][1])
                prev_piece_cast = D(
                    "vector", lambda e, dstb=dstb, par=par, q=q: e.tensor_copy(
                        dstb[:, par, q * 2:(q + 1) * 2, :], wstg[:, q % 2]),
                    inc="vC")
                piece_by_par[gq2 % 2] = prev_piece_cast
            wcast[(oc, wi)] = prev_piece_cast
        wv_cast = wcast[(oc, 1)]
        for t in range(NT):
            kb_, vb_ = t % 2, 2 + t % 2
            tpar = t % 2
            if t == 0:
                W("tensor", "vC", wv_cast)
            if oc == 0 and t == 0:
                W("tensor", "vC", p1_last_ev)
            if t >= 2:
                W("tensor", "vC", p2_kev[(oc, t - 2)])
                W("tensor", "aC", p2_vev[(oc, t - 2)])
            elif oc >= 1:
                W("tensor", "vC", p2_kev[(oc - 1, NT - 2 + t)])
                W("tensor", "aC", p2_vev[(oc - 1, NT - 2 + t)])
            if t == 0 and oc >= 2:
                W("tensor", "vC", kv_ev[oc - 2])
            for j in range(NJ):
                D("tensor", lambda e, j=j, t=t, kb_=kb_, par=par: e.matmul(
                    pbank(kb_), xnT[:, j, t * 128:(t + 1) * 128],
                    wkb[:, par, j, :],
                    start=(j == 0), stop=(j == NJ - 1)),
                    **(dict(inc="tC") if j == NJ - 1 else {}))
            kproj = P.c("tC")
            for j in range(NJ):
                D("tensor", lambda e, j=j, t=t, vb_=vb_, par=par: e.matmul(
                    pbank(vb_), xnT[:, j, t * 128:(t + 1) * 128],
                    wvb[:, par, j, :],
                    start=(j == 0), stop=(j == NJ - 1)),
                    **(dict(inc="tC") if j == NJ - 1 else {}))
            vproj = P.c("tC")
            W("vector", "tC", kproj)
            e1 = D("vector", lambda e, kb_=kb_, tpar=tpar: e.tensor_scalar(
                elut[:, tpar], pbank(kb_), 0.0, None, ALU.min), inc="vC")
            W("scalar", "vC", e1)
            e2 = D("scalar", lambda e, tpar=tpar: e.activation(
                elue[:, tpar], elut[:, tpar], AF.Exp), inc="aC")
            W("vector", "aC", e2)
            p2_kev[(oc, t)] = D(
                "vector", lambda e, kb_=kb_, tpar=tpar: e.scalar_tensor_tensor(
                    kbuf[:, tpar], pbank(kb_), 0.0, elue[:, tpar],
                    ALU.max, ALU.add), inc="vC")
            W("scalar", "tC", vproj)
            p2_vev[(oc, t)] = D(
                "scalar", lambda e, vb_=vb_, tpar=tpar: e.activation(
                    vbuf[:, tpar], pbank(vb_), AF.Identity), inc="aC")

            def kv_group(oc, t):
                tpar = t % 2
                W("tensor", "vC", p2_kev[(oc, t)])
                W("tensor", "aC", p2_vev[(oc, t)])
                kvb = 4 if oc % 2 == 0 else 6
                for hh in range(4):
                    D("tensor", lambda e, hh=hh, t=t, kvb=kvb, tpar=tpar:
                      e.matmul(
                          ps[:, kvb * 512 + hh * 128:
                             kvb * 512 + (hh + 1) * 128],
                          kbuf[:, tpar, hh * 128:(hh + 1) * 128],
                          vbuf[:, tpar, hh * 128:(hh + 1) * 128],
                          start=(t == 0 and hh == 0),
                          stop=(t == NT - 1 and hh == 3)))
                    h = oc * 4 + hh
                    D("tensor", lambda e, h=h, t=t, tpar=tpar, hh=hh, oc=oc:
                      e.matmul(
                          ps[:, 5 * 512 + h: 5 * 512 + h + 1],
                          kbuf[:, tpar, hh * 128:(hh + 1) * 128], onesb[:],
                          start=(oc == 0 and t == 0 and hh == 0),
                          stop=(oc == 3 and t == NT - 1 and hh == 3)),
                          **(dict(inc="tC") if hh == 3 else {}))
                kvmm[(oc, t)] = P.c("tC")
            if t >= 1:
                kv_group(oc, t - 1)
            if t == NT - 1:
                kv_group(oc, t)
        W("vector", "tC", kvmm[(oc, NT - 1)])
        kv_ev[oc] = D("vector", lambda e, kvb=4 if oc % 2 == 0 else 6, oc=oc:
                      e.tensor_copy(kvsb[:, oc * 4:(oc + 1) * 4, :],
                                    pbank(kvb).rearrange("p (a c) -> p a c", a=4)),
                      inc="vC")
    W("vector", "tC", kvmm[(3, NT - 1)])
    ks_ev = D("vector", lambda e: e.tensor_copy(
        ksb[:], ps[:, 5 * 512: 5 * 512 + 16]), inc="vC")

    W("gpsimd", "vC", ks_ev)
    W("gpsimd", "vC", kv_ev[3])
    D("gpsimd", lambda e: e.dma_start(
        out=kvin_d[:, 0:2048], in_=kvsb[:].rearrange("p a c -> p (a c)")),
        inc="sKv", n=16)
    g2 = D("gpsimd", lambda e: e.dma_start(
        out=kvin_d[:, 2048:2064], in_=ksb[:]), inc="sKv", n=16)
    W("gpsimd", "sKv", g2)
    cc1_done = D("gpsimd", lambda e: e.collective_compute(
        "AllReduce", ALU.add, replica_groups=[[0, 1], [2, 3], [4, 5], [6, 7]],
        ins=[kvin_d.opt()], outs=[kvout_d.opt()]), inc="cc1")
    W("gpsimd", "cc1", cc1_done)
    D("gpsimd", lambda e: e.dma_start(
        out=kvsb[:].rearrange("p a c -> p (a c)"), in_=kvout_d[:, 0:2048]),
        inc="sKv", n=16)
    g3 = D("gpsimd", lambda e: e.dma_start(
        out=ksb[:], in_=kvout_d[:, 2048:2064]), inc="sKv", n=16)
    W("vector", "sKv", g3)
    kvc1 = D("vector", lambda e: e.tensor_copy(
        kvbf[:, :, 0:128], kvsb[:]), inc="vC")
    kvbf_c = D("vector", lambda e: e.tensor_copy(
        kvbf[:, :, 128:129], ksb[:].rearrange("p (a c) -> p a c", c=1)),
        inc="vC")

    ws_abs = {}
    for i in range(33):
        wsem = "sWS" if i % 2 == 0 else "sWSb"
        if i >= 2:
            W("gpsimd", "aC", ws_abs[i - 2])
        elif i == 0:
            W("gpsimd", "vC", kvbf_c)
        d = D("gpsimd", lambda e, i=i: e.dma_start(
            out=wsst[:, i % 2], in_=wsc_d[:, i * 1024:(i + 1) * 1024]),
            inc=wsem, n=16)
        W("scalar", wsem, d)
        ws_abs[i] = D("scalar", lambda e, i=i: e.activation(
            xsqd[:, 0:1024] if False else wsst[:, i % 2], wsst[:, i % 2],
            AF.Abs, accum_out=wsabs[:, i:i + 1]), inc="aC")
    W("vector", "aC", ws_abs[32])
    ws_sum = {}
    for w in range(3):
        t = D("vector", lambda e, w=w: e.reduce_sum(
            swcols[:, w:w + 1], wsabs[:, w * 11:(w + 1) * 11], axis=AX),
            inc="vC")
        W("vector", "vC", t)
        ws_sum[w] = t
    ws_done = ws_sum[2]
    W("gpsimd", "vC", ws_done)
    g1 = D("gpsimd", lambda e: e.dma_start(out=wsin_d, in_=swcols[:, 0:4]),
           inc="sM", n=16)
    W("gpsimd", "sM", g1)
    cc2_done = D("gpsimd", lambda e: e.collective_compute(
        "AllReduce", ALU.add, replica_groups=[list(range(8))],
        ins=[wsin_d.opt()], outs=[wsout_d.opt()]), inc="cc2")
    W("gpsimd", "cc2", cc2_done)
    ws_back = D("gpsimd", lambda e: e.dma_start(out=swcols[:, 0:4], in_=wsout_d),
                inc="sM", n=16)

    q_ev, wq_cast, qmm = {}, {}, {}
    for ot in range(NJ):
        wsemq = "sW" if ot % 2 == 0 else "sW2"
        if ot == 0:
            W("sync", "aC", p1_xn[NT - 1])
            W("sync", "vC", wv_cast)
        if ot >= 2:
            W("sync", "vC", wq_cast[ot - 2])
        for half in range(2):
            D("sync", lambda e, ot=ot, half=half: e.dma_start(
                out=wfst[:, ot % 2, half * 8:(half + 1) * 8, :],
                in_=wqt_d[:, ot * 128:(ot + 1) * 128].rearrange(
                    "(j p) o -> p j o", p=128)[:, half * 8:(half + 1) * 8, :]),
                inc=wsemq, n=16)
        W("vector", wsemq, P.c(wsemq))
        if ot >= 2:
            W("vector", "tC", qmm[ot - 2])
        wq_cast[ot] = D("vector", lambda e, ot=ot: e.tensor_copy(
            wqb[:, ot % 2], wfst[:, ot % 2]), inc="vC")
        for tc in range(4):
            gi = ot * 4 + tc
            b = gi % 8
            par = gi % 2
            if tc == 0:
                W("tensor", "vC", wq_cast[ot])
            if gi >= 8:
                W("tensor", "vC", q_ev[gi - 8])
            for j in range(NJ):
                D("tensor", lambda e, j=j, ot=ot, tc=tc, b=b: e.matmul(
                    pbank(b), wqb[:, ot % 2, j, :],
                    xnT[:, j, tc * 512:(tc + 1) * 512],
                    start=(j == 0), stop=(j == NJ - 1)),
                    **(dict(inc="tC") if j == NJ - 1 else {}))
            qmm[ot] = P.c("tC")
            W("vector", "tC", qmm[ot])
            e1 = D("vector", lambda e, b=b, par=par: e.tensor_scalar(
                elut[:, par], pbank(b), 0.0, None, ALU.min), inc="vC")
            W("scalar", "vC", e1)
            e2 = D("scalar", lambda e, par=par: e.activation(
                elue[:, par], elut[:, par], AF.Exp), inc="aC")
            W("vector", "aC", e2)
            q_ev[gi] = D("vector", lambda e, b=b, par=par, ot=ot, tc=tc:
                         e.scalar_tensor_tensor(
                             qT[:, ot, tc * 512:(tc + 1) * 512], pbank(b), 0.0,
                             elue[:, par], ALU.max, ALU.add), inc="vC")
    p3_last = q_ev[63]

    W("tensor", "vC", kvbf_c)
    W("tensor", "vC", p3_last)
    p4_z, p4_tev, p4_trmm = {}, {}, {}
    for tc in range(NT):
        par = tc % 2
        ob0 = 0 if par == 0 else 4
        dbk = 4 if par == 0 else 0
        if tc >= 2:
            W("tensor", "vC", p4_tev[tc - 2][-1])
        if tc >= 1:
            W("tensor", "vC", p4_tev[tc - 1][0])
            W("tensor", "vC", p4_z[tc - 1])
        for h in range(NH):
            D("tensor", lambda e, h=h, tc=tc, dbk=dbk: e.matmul(
                ps[:, dbk * 512 + h: dbk * 512 + h + 1],
                qT[:, h, tc * 128:(tc + 1) * 128], kvbf[:, h, 128:129],
                start=(h == 0), stop=(h == NH - 1)),
                **(dict(inc="tC") if h == NH - 1 else {}))
        den_mm = P.c("tC")
        for h in range(NH):
            b = ob0 + h // 4
            D("tensor", lambda e, h=h, tc=tc, b=b: e.matmul(
                ps[:, b * 512 + (h % 4) * 128: b * 512 + (h % 4) * 128 + 128],
                qT[:, h, tc * 128:(tc + 1) * 128], kvbf[:, h, 0:128],
                start=(h % 4 == 0), stop=(h % 4 == 3)),
                **(dict(inc="tC") if h == NH - 1 else {}))
        out_mm = P.c("tC")
        W("vector", "tC", den_mm)
        z1 = D("vector", lambda e, dbk=dbk, par=par: e.tensor_scalar(
            zsb[:, par], ps[:, dbk * 512: dbk * 512 + 16], 1e-6, None, ALU.add),
            inc="vC")
        W("vector", "vC", z1)
        p4_z[tc] = D("vector", lambda e, par=par: e.reciprocal(
            zsb[:, par], zsb[:, par]), inc="vC")
        W("vector", "tC", out_mm)
        W("vector", "vC", p4_z[tc])
        if tc == 0:
            W("vector", "tC", kvmm[(3, NT - 1)])
        if tc >= 2:
            W("vector", "tC", p4_trmm[tc - 2])
        W("scalar", "tC", out_mm)
        W("scalar", "vC", p4_z[tc])
        if tc == 0:
            W("scalar", "tC", kvmm[(3, NT - 1)])
        if tc >= 2:
            W("scalar", "tC", p4_trmm[tc - 2])
        for h in range(8):
            b = ob0 + h // 4
            kw = dict(inc="aC") if h == 7 else {}
            D("scalar", lambda e, h=h, b=b, par=par: e.activation(
                apre[:, par, h * 128:(h + 1) * 128],
                ps[:, b * 512 + (h % 4) * 128: b * 512 + (h % 4) * 128 + 128],
                AF.Identity, scale=zsb[:, par, h:h + 1]), **kw)
        aev_a = P.c("aC")
        for h in range(8, NH):
            b = ob0 + h // 4
            kw = dict(inc="vC") if h == NH - 1 else {}
            D("vector", lambda e, h=h, b=b, par=par: e.tensor_scalar(
                apre[:, par, h * 128:(h + 1) * 128],
                ps[:, b * 512 + (h % 4) * 128: b * 512 + (h % 4) * 128 + 128],
                zsb[:, par, h:h + 1], None, ALU.mult), **kw)
        aev = P.c("vC")
        W("tensor", "vC", aev)
        W("tensor", "aC", aev_a)
        evs = []
        for g in range(4):
            b = ob0 + g
            for jj in range(4):
                j = g * 4 + jj
                kw = dict(inc="tC") if jj == 3 else {}
                D("tensor", lambda e, j=j, b=b, jj=jj, par=par: e.matmul(
                    ps[:, b * 512 + jj * 128: b * 512 + jj * 128 + 128],
                    apre[:, par, j * 128:(j + 1) * 128], identb[:],
                    start=(jj == 0), stop=(jj == 3)), **kw)
            W("vector", "tC", P.c("tC"))
            evs.append(D("vector", lambda e, g=g, b=b, tc=tc: e.tensor_copy(
                ApreT[:, g * 4:(g + 1) * 4, tc * 128:(tc + 1) * 128],
                pbank(b).rearrange("p (a c) -> p a c", a=4)), inc="vC"))
        p4_trmm[tc] = P.c("tC")
        p4_tev[tc] = evs
    p4_last_ev = p4_tev[NT - 1][-1]
    p4_prev_ev = p4_tev[NT - 2][-1]
    p4_end_mm = P.c("tC")

    W("tensor", "vC", p4_last_ev)
    W("tensor", "vC", p4_prev_ev)
    W("gpsimd", "vC", wq_cast[NJ - 1])
    W("gpsimd", "aC", p1_xn[NT - 1])
    p1b_ka, p1b_k2, p1b_tr = {}, {}, {}
    trsb = {"gidx": 0, "ev": {}}

    def p1b_transpose_tile(src_ap, tcol):
        for g in range(4):
            gi = trsb["gidx"] + g
            b = gi % 4
            if gi - 4 in trsb["ev"]:
                W("tensor", "vC", trsb["ev"][gi - 4])
            for jj in range(4):
                j = g * 4 + jj
                kw = dict(inc="tC") if jj == 3 else {}
                D("tensor", lambda e, j=j, b=b, jj=jj: e.matmul(
                    ps[:, b * 512 + jj * 128: b * 512 + jj * 128 + 128],
                    src_ap[:, j * 128:(j + 1) * 128], identb[:],
                    start=(jj == 0), stop=(jj == 3)), **kw)
            W("vector", "tC", P.c("tC"))
            trsb["ev"][gi] = D("vector", lambda e, g=g, b=b, tcol=tcol:
                               e.tensor_copy(
                                   KTs[:, g * 4:(g + 1) * 4, tcol:tcol + 128],
                                   pbank(b).rearrange("p (a c) -> p a c", a=4)),
                               inc="vC")
        trsb["gidx"] += 4
        return P.c("tC"), trsb["ev"][trsb["gidx"] - 1]

    def p1b_tile(i):
        if i >= 2:
            W("gpsimd", "aC", p1b_ka[i - 2])
        d = D("gpsimd", lambda e, i=i: e.dma_start(
            out=xtile[:, i % 2], in_=x_d[i * 128:(i + 1) * 128, :]),
            inc="sXa" if i % 2 == 0 else "sXb", n=16)
        W("scalar", "sXa" if i % 2 == 0 else "sXb", d)
        if i == 0:
            W("scalar", "aC", ws_abs[32])
        if i >= 1:
            W("scalar", "vC", p1b_k2[i - 1])
        p1b_ka[i] = D("scalar", lambda e, i=i: e.activation(
            kpre[:], xtile[:, i % 2], AF.Identity,
            bias=st(S_NMRSA, i), scale=st(S_RSA, i)), inc="aC")
        W("vector", "aC", p1b_ka[i])
        if i == 0:
            W("vector", "tC", p4_end_mm)
        if i >= 2:
            W("vector", "tC", p1b_tr[i - 2][0])
        k1 = D("vector", lambda e, i=i: e.tensor_scalar(
            kpre[:], kpre[:], MAGIC, MAGIC, ALU.add, ALU.subtract), inc="vC")
        W("vector", "vC", k1)
        p1b_k2[i] = D("vector", lambda e, i=i: e.tensor_scalar(
            ktok[:, i % 2], kpre[:], st(S_DQA, i), None, ALU.mult), inc="vC")
        W("tensor", "vC", p1b_k2[i])
        p1b_tr[i] = p1b_transpose_tile(ktok[:, i % 2], i * 128)

    wo_cast, p5_ev, p5_st, p5_mm, p5_q = {}, {}, {}, {}, {}
    NOC = 4

    def p5_stage(oc):
        for q in range(8):
            gq = oc * 8 + q
            wsemo = "sW" if gq % 2 == 0 else "sW2"
            if gq == 0:
                W("sync", "vC", wq_cast[NJ - 1])
                W("sync", "vC", wq_cast[NJ - 2])
            if gq >= 2:
                W("sync", "aC", p5_q[gq - 2])
            D("sync", lambda e, oc=oc, q=q: e.dma_start(
                out=wostg[:, q % 2],
                in_=wot_d[:, oc * 512:(oc + 1) * 512].rearrange(
                    "(j p) o -> p j o", p=128)[:, q * 2:(q + 1) * 2, :]),
                inc=wsemo, n=16)
            dW = P.c(wsemo)
            W("scalar", wsemo, dW)
            if gq == 0:
                W("scalar", "tC", qmm[NJ - 1])
                W("scalar", "vC", p3_last)
            if q == 0 and oc >= 1:
                W("scalar", "tC", p5_mm[oc - 1])
            p5_q[gq] = D("scalar", lambda e, q=q: e.activation(
                wob[:, q * 2:(q + 1) * 2, :].rearrange("p a b -> p (a b)"),
                wostg[:, q % 2].rearrange("p a b -> p (a b)"), AF.Identity),
                inc="aC")
        wo_cast[oc] = p5_q[oc * 8 + 7]

    def p5_group(oc, tc):
        gi = oc * NT + tc
        b = 4 + gi % 4
        if tc == 0:
            W("tensor", "aC", wo_cast[oc])
        if gi >= 4:
            W("tensor", "aC", p5_ev[gi - 4])
        for j in range(NJ):
            D("tensor", lambda e, j=j, tc=tc, b=b: e.matmul(
                pbank(b), ApreT[:, j, tc * 128:(tc + 1) * 128],
                wob[:, j, :], start=(j == 0), stop=(j == NJ - 1)),
                **(dict(inc="tC") if j == NJ - 1 else {}))
        p5_mm[oc] = P.c("tC")
        W("scalar", "tC", p5_mm[oc])
        if gi >= 2:
            W("scalar", "sYa" if gi % 2 == 0 else "sYb", p5_st[gi - 2])
        p5_ev[gi] = D("scalar", lambda e, b=b, par=gi % 2: e.activation(
            y2pc[:, par], pbank(b), AF.Identity), inc="aC")
        W("gpsimd", "aC", p5_ev[gi])
        p5_st[gi] = D("gpsimd", lambda e, tc=tc, oc=oc, par=gi % 2:
                      e.dma_start(
                          out=y2_d[tc * 128:(tc + 1) * 128,
                                   oc * 512:(oc + 1) * 512],
                          in_=y2pc[:, par]), inc="sYa" if gi % 2 == 0
                      else "sYb", n=16)

    for u in range(16):
        if u % 4 == 0:
            p5_stage(u // 4)
        p1b_tile(u)
        for tcq in range(4):
            p5_group(u // 4, (u % 4) * 4 + tcq)
    p1b_last = p1b_tr[NT - 1][1]
    p5_final_mm = P.c("tC")
    p5_final_aev = P.c("aC")

    W("tensor", "sM", ws_back)
    W("tensor", "aC", p5_final_aev)
    D("tensor", lambda e: e.matmul(
        ps[0:1, 7 * 512: 7 * 512 + 4], onesf_c[:], swcols[:, 0:4],
        start=True, stop=True), inc="tC")
    t_ws = P.c("tC")
    W("scalar", "tC", t_ws)
    a_ws = D("scalar", lambda e: e.activation(
        swrow[:, 0:4], ps[0:1, 7 * 512: 7 * 512 + 4], AF.Identity), inc="aC")
    W("vector", "aC", a_ws)
    r1 = D("vector", lambda e: e.tensor_scalar(
        swrow[:, 4:8], swrow[:, 0:4], 1.0 / (HID * CDIM), 1e-5,
        ALU.mult, ALU.max), inc="vC")
    W("vector", "vC", r1)
    r2 = D("vector", lambda e: e.reciprocal(swrow[:, 8:12], swrow[:, 4:8]),
           inc="vC")
    W("vector", "vC", r2)
    W("tensor", "vC", r2)
    for w in range(3):
        D("tensor", lambda e, w=w: e.matmul(
            ps[:, 7 * 512 + 16 + w: 7 * 512 + 16 + w + 1],
            onesf_r[:], swrow[:, 8 + w: 8 + w + 1], start=(w == 0),
            stop=False))
        D("tensor", lambda e, w=w: e.matmul(
            ps[:, 7 * 512 + 24 + w: 7 * 512 + 24 + w + 1],
            onesf_r[:], swrow[:, 4 + w: 4 + w + 1], start=False,
            stop=(w == 2)),
            **(dict(inc="tC") if w == 2 else {}))
    t_bc = P.c("tC")
    W("vector", "tC", t_bc)
    sw_c1 = D("vector", lambda e: e.tensor_copy(
        swcols[:, 4:7], ps[:, 7 * 512 + 16: 7 * 512 + 19]), inc="vC")
    sw_c2 = D("vector", lambda e: e.tensor_copy(
        swcols[:, 9:12], ps[:, 7 * 512 + 24: 7 * 512 + 27]), inc="vC")

    W("vector", "tC", p5_final_mm)
    mx_init = D("vector", lambda e: e.memset(mxacc[:], 0.0), inc="vC")

    fa_t1, fa_t2, fa_hd, fa_mm, hst, fa_q2 = {}, {}, {}, {}, {}, {}
    fa_q1 = {}

    def emit_fa_quant(m):
        wsem = "sW" if m % 2 == 0 else "sW2"
        if m == 0:
            W("sync", "aC", p1b_ka[NT - 1])
            W("sync", "aC", p5_q[NOC * 8 - 1])
        if m >= 2:
            W("sync", "vC", fa_q1[m - 2])
        for half in range(2):
            D("sync", lambda e, m=m, half=half: e.dma_start(
                out=wfst[:, m % 2, half * 8:(half + 1) * 8, :],
                in_=w1t_d[:, m * 128:(m + 1) * 128].rearrange(
                    "(j p) h -> p j h", p=128)[:, half * 8:(half + 1) * 8, :]),
                inc=wsem, n=16)
        d1 = P.c(wsem)
        W("vector", wsem, d1)
        if m == 0:
            W("vector", "vC", sw_c2)
            W("vector", "tC", p1b_tr[NT - 1][0])
        if m >= 2:
            W("vector", "tC", fa_mm[m - 2])
        wa = wfst[:, m % 2].rearrange("p a b -> p (a b)")
        q1 = D("vector", lambda e, wa=wa: e.tensor_scalar(
            qtmp[:], wa, swcols[:, 4:5], MAGIC, ALU.mult, ALU.add), inc="vC")
        fa_q1[m] = q1
        W("vector", "vC", q1)
        c1v = D("vector", lambda e: e.tensor_scalar(
            qtmp[:], qtmp[:], MAGIC - 1.0, MAGIC + 1.0, ALU.max, ALU.min),
            inc="vC")
        W("vector", "vC", c1v)
        fa_t1[m] = D("vector", lambda e, m=m: e.tensor_scalar(
            t1b[:, m % 2].rearrange("p a b -> p (a b)"), qtmp[:],
            MAGIC, None, ALU.subtract), inc="vC")
        if m == 0:
            W("sync", "tC", p5_final_mm)
        wsem2c = "sF2a" if m % 2 == 0 else "sF2b"
        if m >= 2:
            W("sync", "vC", fa_q2[m - 2])
        for half in range(2):
            D("sync", lambda e, m=m, half=half: e.dma_start(
                out=wfst2[:, m % 2, half * 8:(half + 1) * 8, :],
                in_=w2t_d[:, m * 128:(m + 1) * 128].rearrange(
                    "(j p) h -> p j h", p=128)[:, half * 8:(half + 1) * 8, :]),
                inc=wsem2c, n=16)
        d2 = P.c(wsem2c)
        W("vector", wsem2c, d2)
        wa2 = wfst2[:, m % 2].rearrange("p a b -> p (a b)")
        if m == 0:
            W("vector", "tC", p5_final_mm)
        q2 = D("vector", lambda e, wa2=wa2: e.tensor_scalar(
            qtm2[:], wa2, swcols[:, 5:6], MAGIC, ALU.mult, ALU.add), inc="vC")
        fa_q2[m] = q2
        W("vector", "vC", q2)
        c2v = D("vector", lambda e: e.tensor_scalar(
            qtm2[:], qtm2[:], MAGIC - 1.0, MAGIC + 1.0, ALU.max, ALU.min),
            inc="vC")
        W("vector", "vC", c2v)
        fa_t2[m] = D("vector", lambda e, m=m: e.tensor_scalar(
            t2b[:, m % 2].rearrange("p a b -> p (a b)"), qtm2[:],
            MAGIC, None, ALU.subtract), inc="vC")

    def emit_fa_pe(m):
        for tc in range(4):
            it = m * 4 + tc
            by1, by2 = it % 4, 4 + it % 4
            if tc == 0:
                W("tensor", "vC", fa_t2[m])
            if it == 0:
                W("tensor", "vC", mx_init)
            if it >= 4:
                W("tensor", "aC", fa_hd[it - 4][0])
                W("tensor", "vC", fa_hd[it - 4][1])
            for j in range(NJ):
                D("tensor", lambda e, j=j, m=m, tc=tc, by1=by1: e.matmul(
                    pbank(by1), t1b[:, m % 2, j, :],
                    KTs[:, j, tc * 512:(tc + 1) * 512],
                    start=(j == 0), stop=(j == NJ - 1)),
                    **(dict(inc="tC") if j == NJ - 1 else {}))
            mm1 = P.c("tC")
            for j in range(NJ):
                D("tensor", lambda e, j=j, m=m, tc=tc, by2=by2: e.matmul(
                    pbank(by2), t2b[:, m % 2, j, :],
                    KTs[:, j, tc * 512:(tc + 1) * 512],
                    start=(j == 0), stop=(j == NJ - 1)),
                    **(dict(inc="tC") if j == NJ - 1 else {}))
            mm2 = P.c("tC")
            fa_mm[m] = mm2
            W("scalar", "tC", mm1)
            if it >= 1:
                W("scalar", "vC", fa_hd[it - 1][1])
            s1 = D("scalar", lambda e, by1=by1: e.activation(
                h1t[:], pbank(by1), AF.Silu, scale=swcols[:, 9:10]), inc="aC")
            W("scalar", "tC", mm2)
            s2 = D("scalar", lambda e, by2=by2: e.activation(
                h2t[:], pbank(by2), AF.Identity, scale=swcols[:, 10:11]),
                inc="aC")
            W("vector", "aC", s2)
            if m >= 2:
                W("vector", "sHa" if m % 2 == 0 else "sHb", hst[m - 2])
            if it == 0:
                W("vector", "sYa", p5_st[62])
                W("vector", "sYb", p5_st[63])
            hv = D("vector", lambda e, m=m, tc=tc: e.tensor_tensor(
                hrow[:, m % 2, tc * 512:(tc + 1) * 512], h1t[:], h2t[:],
                ALU.mult), inc="vC")
            W("scalar", "vC", hv)
            habs = D("scalar", lambda e, m=m, tc=tc: e.activation(
                h1t[:], hrow[:, m % 2, tc * 512:(tc + 1) * 512], AF.Abs),
                inc="aC")
            W("vector", "aC", habs)
            mxv = D("vector", lambda e, tc=tc: e.tensor_tensor(
                mxacc[:, tc * 512:(tc + 1) * 512],
                mxacc[:, tc * 512:(tc + 1) * 512], h1t[:], ALU.max),
                inc="vC")
            fa_hd[it] = (s2, mxv, hv)
        W("gpsimd", "vC", fa_hd[m * 4 + 3][1])
        hst[m] = D("gpsimd", lambda e, m=m: e.dma_start(
            out=h_d[m * 128:(m + 1) * 128, :], in_=hrow[:, m % 2]),
            inc="sHa" if m % 2 == 0 else "sHb", n=16)

    emit_fa_quant(0)
    for m in range(44):
        if m + 1 < 44:
            emit_fa_quant(m + 1)
        emit_fa_pe(m)
    fa_last_mm = fa_mm[43]
    fa_last_aev = P.c("aC")

    W("tensor", "vC", fa_hd[44 * 4 - 1][1])
    W("tensor", "aC", fa_last_aev)
    fold_ev = {}
    for bi in range(16):
        b = bi % 4
        if bi >= 4:
            W("tensor", "vC", fold_ev[bi - 4])
        D("tensor", lambda e, bi=bi, b=b: e.matmul(
            ps[:, b * 512: b * 512 + 128],
            mxacc[:, bi * 128:(bi + 1) * 128], identf[:],
            start=True, stop=True), inc="tC")
        W("vector", "tC", P.c("tC"))
        fold_ev[bi] = D("vector", lambda e, bi=bi, b=b: e.reduce_max(
            st(S_MXH, bi), ps[:, b * 512: b * 512 + 128], axis=AX), inc="vC")
    W("vector", "vC", fold_ev[15])
    c1_ = D("vector", lambda e: e.tensor_scalar(
        st(S_MXH), st(S_MXH), 1e-5, None, ALU.max), inc="vC")
    W("vector", "vC", c1_)
    d3 = D("vector", lambda e: e.tensor_scalar(
        st(S_DQ3), st(S_MXH), 1.0 / 127.0, swcols[:, 11:12],
        ALU.mult, ALU.mult), inc="vC")
    shc = D("vector", lambda e: e.reciprocal(st(S_SH), st(S_MXH)), inc="vC")
    W("vector", "vC", shc)
    shc2 = D("vector", lambda e: e.tensor_scalar(
        st(S_SH), st(S_SH), 127.0, None, ALU.mult), inc="vC")
    W("tensor", "vC", shc2)
    D("tensor", lambda e: e.matmul(
        ps[0:16, 4 * 512: 4 * 512 + 128], st(S_SH), identf[:],
        start=True, stop=True), inc="tC")
    t_sh = P.c("tC")
    W("scalar", "tC", t_sh)
    a_shr = D("scalar", lambda e: e.activation(
        rows[:], ps[0:16, 4 * 512: 4 * 512 + 128], AF.Identity), inc="aC")
    W("gpsimd", "aC", a_shr)
    W("gpsimd", "tC", p1b_tr[NT - 1][0])
    W("gpsimd", "vC", d3)
    g_shr = D("gpsimd", lambda e: e.dma_start(
        out=shr[:], in_=rows[:]), inc="sM", n=16)
    W("tensor", "sM", g_shr)
    bcsh_ev = {}
    for q in range(4):
        bq = 5 + q % 2
        if q >= 2:
            W("tensor", "vC", bcsh_ev[q - 2])
        D("tensor", lambda e, q=q, bq=bq: e.matmul(
            pbank(bq), onesf_r[:], shr[:, q * 512:(q + 1) * 512],
            start=True, stop=True), inc="tC")
        W("vector", "tC", P.c("tC"))
        bcsh_ev[q] = D("vector", lambda e, q=q, bq=bq: e.tensor_copy(
            bcsh[:, q * 512:(q + 1) * 512], pbank(bq)), inc="vC")
    bcsh_done = bcsh_ev[3]

    NMAC_N = 11
    out_st = {}
    fstate = {"f1_prev": None, "fin2": {}, "o3r": {}}

    def emit_final_tile(g2f, tc):
        ti = g2f * 8 + tc
        lg = (g2f + 1) * NMAC_N - 1
        if g2f == 0 and tc == 0:
            W("gpsimd", "aC", wo_cast[NOC - 1])
            W("gpsimd", "sYa", p5_st[62])
            W("gpsimd", "sYb", p5_st[63])
        if (g2f, tc) != (0, 0):
            pst = out_st[(g2f, tc - 1)] if tc >= 1 else out_st[(0, 7)]
            W("gpsimd", "sOa" if (tc - 1) % 2 == 0 else "sOb", pst)
        dx = D("gpsimd", lambda e, ti=ti: e.dma_start(
            out=outb[:], in_=x_d[ti * 128:(ti + 1) * 128, :]),
            inc="sXa", n=16)
        dy = D("gpsimd", lambda e, ti=ti, tc=tc: e.dma_start(
            out=y2in[:, tc % 2], in_=y2_d[ti * 128:(ti + 1) * 128, :]),
            inc="sYl", n=16)
        sm, vv = fb_ev[(lg, tc)]
        W("vector", sm, vv)
        W("vector", "sXa", dx)
        f1 = D("vector", lambda e, tc=tc, ti=ti: e.scalar_tensor_tensor(
            outb[:], out3[:, tc, :], st(S_DQ3, ti),
            outb[:], ALU.mult, ALU.add), inc="vC")
        fstate.setdefault("f1", {})[(g2f, tc)] = f1
        W("gpsimd", "vC", f1)
        W("gpsimd", "sYl", dy)
        fin2 = D("gpsimd", lambda e, tc=tc: e.tensor_tensor(
            outb[:], outb[:], y2in[:, tc % 2], ALU.add), inc="pC")
        W("gpsimd", "pC", fin2)
        out_st[(g2f, tc)] = D("gpsimd", lambda e, ti=ti: e.dma_start(
            out=out_d[ti * 128:(ti + 1) * 128, :], in_=outb[:]),
            inc="sOa" if tc % 2 == 0 else "sOb", n=16)

    prev_mm_by_par = {0: P.c("tC"), 1: P.c("tC")}
    prev_kh_by_par = {0: None, 1: None}
    q3_par = {}
    qbox = {"prev_qh": None, "first_dh": True}
    fb_ev = {}
    fb_kh_all, fb_t3_all = {}, {}
    NSUB = 4

    def emit_quant(gmi):
        g2q = gmi // NMAC_N
        miq = gmi % NMAC_N
        tcoq = g2q * 1024
        par = gmi % 2
        h0 = miq * 512
        for s in range(NSUB):
            wsem3 = ("sW", "sW2", "sF2a", "sF2b")[s]
            if q3_par.get(s) is not None:
                W("sync", "aC", q3_par[s])
            if gmi == 0 and s == 0:
                W("sync", "vC", fa_t2[43])
            if gmi == 0 and s == 3:
                W("sync", "vC", bcsh_done)
            d = D("sync", lambda e, h0=h0, s=s: e.dma_start(
                out=wf3[:, s],
                in_=w3t_d[h0 + s * 128: h0 + (s + 1) * 128, :]),
                inc=wsem3, n=16)
            W("scalar", wsem3, d)
            if gmi == 0 and s == 0:
                W("scalar", "tC", fa_last_mm)
            if qbox.get("t3p", {}).get(s % 2) is not None:
                W("scalar", "vC", qbox["t3p"][s % 2])
            q3 = D("scalar", lambda e, s=s: e.activation(
                qt2[:, s % 2], wf3[:, s], AF.Identity,
                scale=swcols[:, 6:7], bias=pmag[:]), inc="aC")
            q3_par[s] = q3
            W("vector", "aC", q3)
            if s == 0:
                W("vector", "tC", prev_mm_by_par[par])
            c3v = D("vector", lambda e, s=s: e.tensor_scalar(
                qt2[:, s % 2], qt2[:, s % 2], MAGIC - 1.0, MAGIC + 1.0,
                ALU.max, ALU.min), inc="vC")
            W("vector", "vC", c3v)
            fb_t3_all[(gmi, s)] = ("vC", D(
                "vector", lambda e, s=s, par=par: e.tensor_scalar(
                    t3b[:, par, s, :], qt2[:, s % 2], MAGIC, None,
                    ALU.subtract), inc="vC"))
            qbox.setdefault("t3p", {})[s % 2] = fb_t3_all[(gmi, s)][1]
            k = gmi * NSUB + s
            if qbox["first_dh"]:
                W("gpsimd", "tC", fa_last_mm)
                W("gpsimd", "sHa", hst[42])
                W("gpsimd", "sHb", hst[43])
                qbox["first_dh"] = False
            if qbox.get(("qh", k - 4)) is not None:
                W("gpsimd", "pC", qbox[("qh", k - 4)])
            hsem = ("sHl", "sHm", "sHa", "sHb")[k % 4]
            dh = D("gpsimd", lambda e, h0=h0, s=s, tcoq=tcoq, k=k:
                   e.dma_start(
                       out=hinb[:, k % 4, :],
                       in_=h_d[h0 + s * 128: h0 + (s + 1) * 128,
                               tcoq:tcoq + 1024]), inc=hsem, n=16)
            W("gpsimd", hsem, dh)
            W("gpsimd", "vC", bcsh_done)
            if qbox.get(("kh", k - 2)) is not None:
                W("gpsimd", "vC", qbox[("kh", k - 2)])
            qh = D("gpsimd", lambda e, tcoq=tcoq, k=k:
                   e.tensor_tensor(
                       qh2b[:, k % 2], hinb[:, k % 4, :],
                       bcsh[:, tcoq:tcoq + 1024], ALU.mult), inc="pC")
            qbox[("qh", k)] = qh
            W("vector", "pC", qh)
            fb_kh_all[(gmi, s)] = D(
                "vector", lambda e, s=s, par=par, k=k: e.tensor_scalar(
                    khb[:, par, s, :], qh2b[:, k % 2], MAGIC, MAGIC,
                    ALU.add, ALU.subtract), inc="vC")
            qbox[("kh", k)] = fb_kh_all[(gmi, s)]

    def emit_pe(gmi):
        miq = gmi % NMAC_N
        par = gmi % 2
        for tc in range(8):
            gb0 = 0 if tc % 2 == 0 else 4
            if tc == 0:
                W("tensor", "vC", fb_kh_all[(gmi, NSUB - 1)])
                W("tensor", fb_t3_all[(gmi, NSUB - 1)][0],
                  fb_t3_all[(gmi, NSUB - 1)][1])
            if tc >= 2:
                sm, vv = fb_ev[(gmi, tc - 2)]
                W("tensor", sm, vv)
            elif gmi >= 1:
                sm, vv = fb_ev[(gmi - 1, 6 + tc)]
                W("tensor", sm, vv)
            else:
                if tc == 0:
                    W("tensor", "vC", fold_ev[15])
                    W("tensor", "vC", bcsh_done)
                    W("tensor", "aC", a_shr)
            for s in range(NSUB):
                for oq in range(4):
                    kw = (dict(inc="tC")
                          if (s == NSUB - 1 and oq == 3) else {})
                    D("tensor", lambda e, s=s, oq=oq, tc=tc, gb0=gb0,
                      par=par:
                      e.matmul(
                          pbank(gb0 + oq),
                          khb[:, par, s, tc * 128:(tc + 1) * 128],
                          t3b[:, par, s, oq * 512:(oq + 1) * 512],
                          start=(s == 0), stop=(s == NSUB - 1)), **kw)
            fb_mm = P.c("tC")
            if tc == 7:
                prev_mm_by_par[par] = fb_mm
            if gmi == NMAC_N:
                emit_final_tile(0, tc)
            if miq == 0:
                W("scalar", "tC", fb_mm)
                if gmi == NMAC_N:
                    W("scalar", "vC", fstate["f1"][(0, tc)])
                v = D("scalar", lambda e, tc=tc, gb0=gb0: e.activation(
                    out3[:, tc, :], pbank(gb0, 4), AF.Identity),
                    inc="aC")
                fb_ev[(gmi, tc)] = ("aC", v)
            else:
                W("vector", "tC", fb_mm)
                v = D("vector", lambda e, tc=tc, gb0=gb0:
                      e.scalar_tensor_tensor(
                          out3[:, tc, :], pbank(gb0, 4), 1.0,
                          out3[:, tc, :], ALU.mult, ALU.add), inc="vC")
                fb_ev[(gmi, tc)] = ("vC", v)

    emit_quant(0)
    for gmi in range(2 * NMAC_N):
        if gmi + 1 < 2 * NMAC_N:
            emit_quant(gmi + 1)
        emit_pe(gmi)
    for tc in range(8):
        emit_final_tile(1, tc)
    W("gpsimd", "sOa", P.c("sOa"))
    W("gpsimd", "sOb", P.c("sOb"))

    with nc.Block() as block:
        @block.sync
        def _(sync):
            for fn in P.q["sync"]:
                fn(sync)

        @block.vector
        def _(vector):
            for fn in P.q["vector"]:
                fn(vector)

        @block.scalar
        def _(scalar):
            for fn in P.q["scalar"]:
                fn(scalar)

        @block.tensor
        def _(tensor):
            for fn in P.q["tensor"]:
                fn(tensor)

        @block.gpsimd
        def _(gpsimd):
            for fn in P.q["gpsimd"]:
                fn(gpsimd)

    return nc


_NC_CACHE = {}


def kernel(**inputs):
    x = np.asarray(inputs["L0_attn_x"], dtype=np.float32)
    Wq = np.asarray(inputs["L0_attn_Wq"], dtype=np.float32)
    Wk = np.asarray(inputs["L0_attn_Wk"], dtype=np.float32)
    Wv = np.asarray(inputs["L0_attn_Wv"], dtype=np.float32)
    Wo = np.asarray(inputs["L0_attn_Wo"], dtype=np.float32)
    w1 = np.asarray(inputs["L0_ffn_w1"], dtype=np.float32)
    w2 = np.asarray(inputs["L0_ffn_w2"], dtype=np.float32)
    w3 = np.asarray(inputs["L0_ffn_w3"], dtype=np.float32)

    wqt = np.ascontiguousarray(Wq.T)
    wkt = np.ascontiguousarray(Wk.T)
    wvt = np.ascontiguousarray(Wv.T)
    wot = np.ascontiguousarray(Wo.T)
    w1t = np.ascontiguousarray(w1.T)
    w2t = np.ascontiguousarray(w2.T)
    w3t = np.ascontiguousarray(w3.T)
    idf = np.eye(128, dtype=np.float32)

    nsl = HID * CDIM // 8
    in_maps = []
    for c in range(8):
        b, g = c // 2, c % 2
        xs = np.ascontiguousarray(x[b, g * TOK:(g + 1) * TOK, :])
        wsc = np.concatenate(
            [w.reshape(-1)[c * nsl:(c + 1) * nsl].reshape(128, nsl // 128)
             for w in (w1, w2, w3)], axis=1)
        in_maps.append({"x": xs, "wqt": wqt, "wkt": wkt, "wvt": wvt,
                        "wot": wot, "w1t": w1t, "w2t": w2t, "w3t": w3t,
                        "wsc": np.ascontiguousarray(wsc), "idf": idf})

    if "nc" not in _NC_CACHE:
        _NC_CACHE["nc"] = build_nc()
    res = run_bass_kernel_spmd(_NC_CACHE["nc"], in_maps, core_ids=list(range(8)))
    out = np.zeros((4, 4096, CDIM), dtype=np.float32)
    for c in range(8):
        b, g = c // 2, c % 2
        out[b, g * TOK:(g + 1) * TOK, :] = res.results[c]["out"]
    return out


# version 16 (final)
# speedup vs baseline: 1.0605x; 1.0300x over previous
import numpy as np
import contextlib
import concourse.bass as bass
import concourse.mybir as mybir
from concourse.bass_utils import run_bass_kernel_spmd

F32 = mybir.dt.float32
BF16 = mybir.dt.bfloat16
AX = mybir.AxisListType.X
AF = mybir.ActivationFunctionType
ALU = mybir.AluOpType

TOK, CDIM, HID = 2048, 2048, 5632
NT, NJ, NH = 16, 16, 16
MAGIC = 12582912.0

(S_SUM, S_SQ, S_MX, S_MN, S_M, S_VAR, S_R, S_MXC, S_SA, S_DQA, S_RSA,
 S_NMR, S_NMRSA, S_DQ3, S_MXH, S_SH) = range(16)


class Prog:
    ENGINES = ("sync", "vector", "scalar", "tensor", "gpsimd")

    def __init__(self):
        self.q = {e: [] for e in self.ENGINES}
        self.sems = {}
        self.cnt = {}

    def emit(self, eng, fn):
        self.q[eng].append(fn)

    def wait(self, eng, sname, val):
        if val and val > 0:
            self.emit(eng, lambda e, s=self.sems[sname], v=val: e.wait_ge(s, v))

    def do(self, eng, build, inc=None, n=1):
        if inc is None:
            self.emit(eng, build)
            return None
        self.cnt[inc] += n
        sem = self.sems[inc]

        def run(e, b=build, s=sem, k=n):
            b(e).then_inc(s, k)
        self.emit(eng, run)
        return self.cnt[inc]

    def c(self, name):
        return self.cnt[name]


def build_nc():
    nc = bass.Bass("TRN2", target_bir_lowering=False)

    x_d = nc.dram_tensor("x", [TOK, CDIM], F32, kind="ExternalInput").ap()
    wqt_d = nc.dram_tensor("wqt", [CDIM, CDIM], F32, kind="ExternalInput").ap()
    wkt_d = nc.dram_tensor("wkt", [CDIM, CDIM], F32, kind="ExternalInput").ap()
    wvt_d = nc.dram_tensor("wvt", [CDIM, CDIM], F32, kind="ExternalInput").ap()
    wot_d = nc.dram_tensor("wot", [CDIM, CDIM], F32, kind="ExternalInput").ap()
    w1t_d = nc.dram_tensor("w1t", [CDIM, HID], F32, kind="ExternalInput").ap()
    w2t_d = nc.dram_tensor("w2t", [CDIM, HID], F32, kind="ExternalInput").ap()
    w3t_d = nc.dram_tensor("w3t", [HID, CDIM], F32, kind="ExternalInput").ap()
    wsc_d = nc.dram_tensor("wsc", [128, 3 * 11264], F32, kind="ExternalInput").ap()
    idf_d = nc.dram_tensor("idf", [128, 128], F32, kind="ExternalInput").ap()
    out_d = nc.dram_tensor("out", [TOK, CDIM], F32, kind="ExternalOutput").ap()

    h_d = nc.dram_tensor("h_scr", [HID, TOK], BF16).ap()
    y2_d = nc.dram_tensor("y2_scr", [TOK, CDIM], BF16).ap()
    kvin_d = nc.dram_tensor("cc_kv_in", [128, 2064], F32).ap()
    kvout_d = nc.dram_tensor("cc_kv_out", [128, 2064], F32).ap()
    wsin_d = nc.dram_tensor("cc_ws_in", [128, 4], F32).ap()
    wsout_d = nc.dram_tensor("cc_ws_out", [128, 4], F32).ap()

    A0 = 16512
    B0 = A0 + 65536
    C0 = B0 + 65536

    def at(name, shape, dtype, off):
        h = nc.alloc_sbuf_tensor_at(name, shape, dtype, offset=off)
        per = int(np.prod(shape[1:])) * mybir.dt.size(dtype)
        assert off >= 16512 and off + per <= 229344, (name, off, off + per)
        return h

    xnT = at("xnT", [128, NJ, TOK], BF16, A0)
    ApreT = at("ApreT", [128, NJ, TOK], BF16, A0)
    out3 = at("out3", [128, 8, CDIM], F32, A0)
    mxacc = at("mxacc", [128, CDIM], F32, A0)
    h1t = at("h1t", [128, 512], F32, A0 + 8192)
    h2t = at("h2t", [128, 512], F32, A0 + 10240)
    wfst2 = at("wfst2", [128, 2, NJ, 128], F32, A0 + 16384)
    qtm2 = at("qtm2", [128, CDIM], F32, A0 + 32768)
    qT = at("qT", [128, NJ, TOK], BF16, B0)
    KTs = at("KTs", [128, NJ, TOK], BF16, B0)
    wkb = at("wkb", [128, 2, NJ, 512], BF16, B0)
    wvb = at("wvb", [128, 2, NJ, 512], BF16, B0 + 32768)
    t3b = at("t3b", [128, 2, 4, CDIM], BF16, B0)
    khb = at("khb", [128, 2, 4, 1024], BF16, B0 + 32768)
    hinb = at("hinb", [128, 4, 1024], BF16, B0 + 49152)
    qh2b = at("qh2b", [128, 2, 1024], F32, B0 + 57344)
    c0 = C0
    c1 = C0 + 16384
    c2 = C0 + 24576
    c3 = C0 + 32768
    c4 = C0 + 40960
    c5 = C0 + 49152
    c6 = C0 + 65536
    cP = C0 + 73728
    xtile = at("xtile", [128, 2, CDIM], F32, c0)
    wsst = at("wsst", [128, 2, 1024], F32, c6)
    wfst = at("wfst", [128, 2, NJ, 128], F32, c0)
    wf3 = at("wf3", [128, 4, CDIM], F32, c0)
    xsqd = at("xsqd", [128, CDIM], F32, c1)
    kvsb = at("kvsb", [128, NH, 128], F32, c1)
    kpre = at("kpre", [128, CDIM], F32, c1)
    wostg = at("wostg", [128, 2, 2, 512], F32, c3)
    xtl2 = at("xtl2", [128, CDIM], F32, c1)
    kvbf = at("kvbf", [128, NH, 129], BF16, c2)
    wstg = at("wstg", [128, 2, 2, 512], F32, c2)
    ktok = at("ktok", [128, 2, CDIM], BF16, c2)
    qtmp = at("qtmp", [128, CDIM], F32, c2)
    shr = at("shr", [1, CDIM], F32, c2)
    xnb = at("xnb", [128, 2, CDIM], BF16, c3)
    qt2 = at("qt2", [128, 2, CDIM], F32, c5)
    wob = at("wob", [128, NJ, 512], BF16, c5)
    outb = at("outb", [128, CDIM], F32, c3)
    kbuf = at("kbuf", [128, 2, 512], BF16, c4)
    vbuf = at("vbuf", [128, 2, 512], BF16, c4 + 2048)
    apre = at("apre", [128, 2, CDIM], BF16, c4)
    y2pc = at("y2pc", [128, 2, 512], BF16, c4)
    hrow = at("hrow", [128, 2, CDIM], BF16, c4)
    y2in = at("y2in", [128, 2, CDIM], BF16, c4)
    elut = at("elut", [128, 2, 512], F32, c5)
    elue = at("elue", [128, 2, 512], F32, c5 + 4096)
    wqb = at("wqb", [128, 2, NJ, 128], BF16, c5 + 8192)
    t1b = at("t1b", [128, 2, NJ, 128], BF16, c5)
    t2b = at("t2b", [128, 2, NJ, 128], BF16, c5 + 8192)
    bcsh = at("bcsh", [128, CDIM], F32, c6)
    stats = at("stats", [128, 256], F32, cP)
    swcols = at("swcols", [128, 16], F32, cP + 1024)
    swrow = at("swrow", [1, 16], F32, cP + 1088)
    ksb = at("ksb", [128, 16], F32, cP + 1152)
    onesb = at("onesb", [128, 1], BF16, cP + 1216)
    onesf_r = at("onesf_r", [1, 128], F32, cP + 1248)
    onesf_c = at("onesf_c", [128, 1], F32, cP + 1760)
    identf = at("identf", [128, 128], F32, cP + 1792)
    identb = at("identb", [128, 128], BF16, cP + 2304)
    zsb = at("zsb", [128, 2, 16], F32, cP + 2560)
    rows = at("rows", [16, 128], F32, cP + 2688)
    nmag = at("nmag", [128, 1], F32, cP + 3200)
    czero = at("czero", [128, 1], F32, cP + 3232)
    cone = at("cone", [128, 1], F32, cP + 3264)
    pmag = at("pmag", [128, 1], F32, cP + 3296)
    qh2 = at("qh2", [128, 1024], F32, cP + 3328)
    wsabs = at("wsabs", [128, 36], F32, cP + 7424)
    nc.const_aps.register(F32, 0.0, czero[:])
    nc.const_aps.register(F32, 1.0, cone[:])

    P = Prog()
    stack = contextlib.ExitStack()
    ps = stack.enter_context(nc.psum_tensor("ps", [128, 4096], F32))
    for s in ("sW", "sXa", "sXb", "sM", "sHa", "sHb", "sHl", "sYa", "sYb",
              "sYl", "sOa", "sOb", "cc1", "cc2", "vC", "aC", "tC", "sI",
              "sKv", "sHm", "sWS", "pC", "sW2", "sWSb", "sF2a", "sF2b"):
        P.sems[s] = stack.enter_context(nc.semaphore(s))
        P.cnt[s] = 0

    def st(slot, i=None):
        if i is None:
            return stats[:, slot * 16:(slot + 1) * 16]
        return stats[:, slot * 16 + i: slot * 16 + i + 1]

    def pbank(b, w=1):
        return ps[:, b * 512:(b + w) * 512]

    W, D = P.wait, P.do

    D("gpsimd", lambda e: e.dma_start(out=identf[:], in_=idf_d), inc="sI", n=16)
    for fn in (lambda e: e.memset(czero[:], 0.0),
               lambda e: e.memset(cone[:], 1.0),
               lambda e: e.memset(onesb[:], 1.0),
               lambda e: e.memset(onesf_r[:], 1.0),
               lambda e: e.memset(onesf_c[:], 1.0),
               lambda e: e.memset(swcols[:], 0.0),
               lambda e: e.memset(stats[:], 0.0),
               lambda e: e.memset(nmag[:], -MAGIC),
               lambda e: e.memset(pmag[:], MAGIC),
               lambda e: e.memset(wsabs[:], 0.0)):
        t = D("vector", fn, inc="vC")
        W("vector", "vC", t)
    W("vector", "sI", 16)
    v_identb = D("vector", lambda e: e.tensor_copy(identb[:], identf[:]), inc="vC")
    v_init = v_identb
    W("vector", "vC", v_init)

    x0_dma, p0_v, p0_a = {}, {}, {}
    for i in range(NT):
        if i >= 2:
            W("gpsimd", "vC", p0_v[i - 2])
            W("gpsimd", "aC", p0_a[i - 2])
        x0_dma[i] = D("gpsimd", lambda e, i=i: e.dma_start(
            out=xtile[:, i % 2], in_=x_d[i * 128:(i + 1) * 128, :]),
            inc="sXa" if i % 2 == 0 else "sXb", n=16)
        W("scalar", "sXa" if i % 2 == 0 else "sXb", x0_dma[i])
        if i == 0:
            W("scalar", "vC", v_init)
        if i >= 1:
            W("scalar", "aC", p0_a[i - 1])
        aid = D("scalar", lambda e, i=i: e.activation(
            xsqd[:], xtile[:, i % 2], AF.Identity, accum_out=st(S_SUM, i)),
            inc="aC")
        W("scalar", "aC", aid)
        p0_a[i] = D("scalar", lambda e, i=i: e.activation(
            xsqd[:], xtile[:, i % 2], AF.Square, accum_out=st(S_SQ, i)),
            inc="aC")
        W("vector", "sXa" if i % 2 == 0 else "sXb", x0_dma[i])
        D("vector", lambda e, i=i: e.reduce_max(
            st(S_MX, i), xtile[:, i % 2], axis=AX), inc="vC")
        p0_v[i] = D("vector", lambda e, i=i: e.tensor_reduce(
            st(S_MN, i), xtile[:, i % 2], axis=AX, op=ALU.min), inc="vC")

    W("vector", "aC", p0_a[NT - 1])
    W("vector", "vC", p0_v[NT - 1])

    def colop(build):
        t = D("vector", build, inc="vC")
        W("vector", "vC", t)
        return t

    colop(lambda e: e.tensor_scalar(st(S_M), st(S_SUM), 1.0 / CDIM, None, ALU.mult))
    colop(lambda e: e.tensor_scalar(st(S_VAR), st(S_SQ), 1.0 / CDIM, None, ALU.mult))
    colop(lambda e: e.tensor_tensor(st(S_SQ), st(S_M), st(S_M), ALU.mult))
    colop(lambda e: e.tensor_tensor(st(S_VAR), st(S_VAR), st(S_SQ), ALU.subtract))
    v_var = colop(lambda e: e.tensor_scalar(st(S_VAR), st(S_VAR), 1e-5, None,
                                            ALU.add))
    W("scalar", "aC", p0_a[NT - 1])
    W("scalar", "vC", v_var)
    a_sq = D("scalar", lambda e: e.activation(st(S_SQ), st(S_VAR), AF.Sqrt),
             inc="aC")
    W("vector", "aC", a_sq)
    colop(lambda e: e.reciprocal(st(S_R), st(S_SQ)))
    colop(lambda e: e.tensor_tensor(st(S_MX), st(S_MX), st(S_M), ALU.subtract))
    colop(lambda e: e.tensor_tensor(st(S_MN), st(S_M), st(S_MN), ALU.subtract))
    colop(lambda e: e.tensor_tensor(st(S_MX), st(S_MX), st(S_MN), ALU.max))
    colop(lambda e: e.tensor_tensor(st(S_MXC), st(S_MX), st(S_R), ALU.mult))
    colop(lambda e: e.tensor_scalar(st(S_MXC), st(S_MXC), 1e-5, None, ALU.max))
    colop(lambda e: e.reciprocal(st(S_SA), st(S_MXC)))
    colop(lambda e: e.tensor_scalar(st(S_SA), st(S_SA), 127.0, None, ALU.mult))
    colop(lambda e: e.tensor_scalar(st(S_DQA), st(S_MXC), 1.0 / 127.0, None,
                                    ALU.mult))
    colop(lambda e: e.tensor_tensor(st(S_RSA), st(S_R), st(S_SA), ALU.mult))
    colop(lambda e: e.scalar_tensor_tensor(st(S_NMR), st(S_M), -1.0, st(S_R),
                                           ALU.mult, ALU.mult))
    p0_cols = colop(lambda e: e.scalar_tensor_tensor(
        st(S_NMRSA), st(S_M), -1.0, st(S_RSA), ALU.mult, ALU.mult))

    trs = {"gidx": 0, "ev": {}}

    def transpose_tile(src_ap, dst, tcol):
        for g in range(4):
            gi = trs["gidx"] + g
            b = gi % 8
            if gi - 8 in trs["ev"]:
                W("tensor", "vC", trs["ev"][gi - 8])
            for jj in range(4):
                j = g * 4 + jj
                kw = dict(inc="tC") if jj == 3 else {}
                D("tensor", lambda e, j=j, b=b, jj=jj: e.matmul(
                    ps[:, b * 512 + jj * 128: b * 512 + jj * 128 + 128],
                    src_ap[:, j * 128:(j + 1) * 128], identb[:],
                    start=(jj == 0), stop=(jj == 3)), **kw)
            W("vector", "tC", P.c("tC"))
            trs["ev"][gi] = D("vector", lambda e, g=g, b=b: e.tensor_copy(
                dst[:, g * 4:(g + 1) * 4, tcol:tcol + 128],
                pbank(b).rearrange("p (a c) -> p a c", a=4)), inc="vC")
        trs["gidx"] += 4
        return P.c("tC"), trs["ev"][trs["gidx"] - 1]

    W("gpsimd", "vC", p0_v[NT - 1])
    W("gpsimd", "aC", p0_a[NT - 1])
    p1_xn, p1_tr = {}, {}
    for i in range(NT):
        if i >= 2:
            W("gpsimd", "aC", p1_xn[i - 2])
        d = D("gpsimd", lambda e, i=i: e.dma_start(
            out=xtile[:, i % 2], in_=x_d[i * 128:(i + 1) * 128, :]),
            inc="sXa" if i % 2 == 0 else "sXb", n=16)
        W("scalar", "sXa" if i % 2 == 0 else "sXb", d)
        if i == 0:
            W("scalar", "vC", p0_cols)
        if i >= 2:
            W("scalar", "tC", p1_tr[i - 2][0])
        p1_xn[i] = D("scalar", lambda e, i=i: e.activation(
            xnb[:, i % 2], xtile[:, i % 2], AF.Identity,
            bias=st(S_NMR, i), scale=st(S_R, i)), inc="aC")
        W("tensor", "aC", p1_xn[i])
        if i == 0:
            W("tensor", "vC", v_identb)
        p1_tr[i] = transpose_tile(xnb[:, i % 2], xnT, i * 128)
    p1_last_ev = p1_tr[NT - 1][1]

    kv_ev, p2_kev, p2_vev, kvmm = {}, {}, {}, {}
    prev_piece_cast = None
    piece_by_par = {}
    wcast = {}
    ws_abs = {}

    def emit_ws_chunk(i):
        wsem = "sWS" if i % 2 == 0 else "sWSb"
        if i >= 2:
            W("gpsimd", "aC", ws_abs[i - 2])
        d = D("gpsimd", lambda e, i=i: e.dma_start(
            out=wsst[:, i % 2], in_=wsc_d[:, i * 1024:(i + 1) * 1024]),
            inc=wsem, n=16)
        W("scalar", wsem, d)
        ws_abs[i] = D("scalar", lambda e, i=i: e.activation(
            wsst[:, i % 2], wsst[:, i % 2],
            AF.Abs, accum_out=wsabs[:, i:i + 1]), inc="aC")

    for oc in range(4):
        par = oc % 2
        for wi, (wt_d, dstb) in enumerate(((wkt_d, wkb), (wvt_d, wvb))):
            for q in range(8):
                gq2 = (oc * 2 + wi) * 8 + q
                wsem2 = "sW" if gq2 % 2 == 0 else "sW2"
                if piece_by_par.get(gq2 % 2) is not None:
                    W("sync", "vC", piece_by_par[gq2 % 2])
                D("sync", lambda e, wt_d=wt_d, oc=oc, q=q: e.dma_start(
                    out=wstg[:, q % 2],
                    in_=wt_d[:, oc * 512:(oc + 1) * 512].rearrange(
                        "(j p) o -> p j o", p=128)[:, q * 2:(q + 1) * 2, :]),
                    inc=wsem2, n=16)
                W("vector", wsem2, P.c(wsem2))
                if q == 0 and oc >= 2:
                    W("vector", "tC", kvmm[(oc - 2, NT - 1)])
                if q == 0 and oc == 0 and wi == 0:
                    W("vector", "vC", p1_tr[NT - 1][1])
                prev_piece_cast = D(
                    "vector", lambda e, dstb=dstb, par=par, q=q: e.tensor_copy(
                        dstb[:, par, q * 2:(q + 1) * 2, :], wstg[:, q % 2]),
                    inc="vC")
                piece_by_par[gq2 % 2] = prev_piece_cast
            wcast[(oc, wi)] = prev_piece_cast
        wv_cast = wcast[(oc, 1)]
        for t in range(NT):
            kb_, vb_ = t % 2, 2 + t % 2
            tpar = t % 2
            if t == 0:
                W("tensor", "vC", wv_cast)
            if oc == 0 and t == 0:
                W("tensor", "vC", p1_last_ev)
            if t >= 2:
                W("tensor", "vC", p2_kev[(oc, t - 2)])
                W("tensor", "aC", p2_vev[(oc, t - 2)])
            elif oc >= 1:
                W("tensor", "vC", p2_kev[(oc - 1, NT - 2 + t)])
                W("tensor", "aC", p2_vev[(oc - 1, NT - 2 + t)])
            if t == 0 and oc >= 2:
                W("tensor", "vC", kv_ev[oc - 2])
            for j in range(NJ):
                D("tensor", lambda e, j=j, t=t, kb_=kb_, par=par: e.matmul(
                    pbank(kb_), xnT[:, j, t * 128:(t + 1) * 128],
                    wkb[:, par, j, :],
                    start=(j == 0), stop=(j == NJ - 1)),
                    **(dict(inc="tC") if j == NJ - 1 else {}))
            kproj = P.c("tC")
            for j in range(NJ):
                D("tensor", lambda e, j=j, t=t, vb_=vb_, par=par: e.matmul(
                    pbank(vb_), xnT[:, j, t * 128:(t + 1) * 128],
                    wvb[:, par, j, :],
                    start=(j == 0), stop=(j == NJ - 1)),
                    **(dict(inc="tC") if j == NJ - 1 else {}))
            vproj = P.c("tC")
            W("vector", "tC", kproj)
            e1 = D("vector", lambda e, kb_=kb_, tpar=tpar: e.tensor_scalar(
                elut[:, tpar], pbank(kb_), 0.0, None, ALU.min), inc="vC")
            W("scalar", "vC", e1)
            e2 = D("scalar", lambda e, tpar=tpar: e.activation(
                elue[:, tpar], elut[:, tpar], AF.Exp), inc="aC")
            W("vector", "aC", e2)
            p2_kev[(oc, t)] = D(
                "vector", lambda e, kb_=kb_, tpar=tpar: e.scalar_tensor_tensor(
                    kbuf[:, tpar], pbank(kb_), 0.0, elue[:, tpar],
                    ALU.max, ALU.add), inc="vC")
            W("scalar", "tC", vproj)
            p2_vev[(oc, t)] = D(
                "scalar", lambda e, vb_=vb_, tpar=tpar: e.activation(
                    vbuf[:, tpar], pbank(vb_), AF.Identity), inc="aC")

            def kv_group(oc, t):
                tpar = t % 2
                W("tensor", "vC", p2_kev[(oc, t)])
                W("tensor", "aC", p2_vev[(oc, t)])
                kvb = 4 if oc % 2 == 0 else 6
                for hh in range(4):
                    D("tensor", lambda e, hh=hh, t=t, kvb=kvb, tpar=tpar:
                      e.matmul(
                          ps[:, kvb * 512 + hh * 128:
                             kvb * 512 + (hh + 1) * 128],
                          kbuf[:, tpar, hh * 128:(hh + 1) * 128],
                          vbuf[:, tpar, hh * 128:(hh + 1) * 128],
                          start=(t == 0 and hh == 0),
                          stop=(t == NT - 1 and hh == 3)))
                    h = oc * 4 + hh
                    D("tensor", lambda e, h=h, t=t, tpar=tpar, hh=hh, oc=oc:
                      e.matmul(
                          ps[:, 5 * 512 + h: 5 * 512 + h + 1],
                          kbuf[:, tpar, hh * 128:(hh + 1) * 128], onesb[:],
                          start=(oc == 0 and t == 0 and hh == 0),
                          stop=(oc == 3 and t == NT - 1 and hh == 3)),
                          **(dict(inc="tC") if hh == 3 else {}))
                kvmm[(oc, t)] = P.c("tC")
            if t >= 1:
                kv_group(oc, t - 1)
            if t == NT - 1:
                kv_group(oc, t)
            if t % 2 == 1:
                emit_ws_chunk(oc * 8 + t // 2)
        W("vector", "tC", kvmm[(oc, NT - 1)])
        kv_ev[oc] = D("vector", lambda e, kvb=4 if oc % 2 == 0 else 6, oc=oc:
                      e.tensor_copy(kvsb[:, oc * 4:(oc + 1) * 4, :],
                                    pbank(kvb).rearrange("p (a c) -> p a c", a=4)),
                      inc="vC")
    W("vector", "tC", kvmm[(3, NT - 1)])
    ks_ev = D("vector", lambda e: e.tensor_copy(
        ksb[:], ps[:, 5 * 512: 5 * 512 + 16]), inc="vC")

    W("gpsimd", "vC", ks_ev)
    W("gpsimd", "vC", kv_ev[3])
    D("gpsimd", lambda e: e.dma_start(
        out=kvin_d[:, 0:2048], in_=kvsb[:].rearrange("p a c -> p (a c)")),
        inc="sKv", n=16)
    g2 = D("gpsimd", lambda e: e.dma_start(
        out=kvin_d[:, 2048:2064], in_=ksb[:]), inc="sKv", n=16)
    W("gpsimd", "sKv", g2)
    cc1_done = D("gpsimd", lambda e: e.collective_compute(
        "AllReduce", ALU.add, replica_groups=[[0, 1], [2, 3], [4, 5], [6, 7]],
        ins=[kvin_d.opt()], outs=[kvout_d.opt()]), inc="cc1")
    W("gpsimd", "cc1", cc1_done)
    D("gpsimd", lambda e: e.dma_start(
        out=kvsb[:].rearrange("p a c -> p (a c)"), in_=kvout_d[:, 0:2048]),
        inc="sKv", n=16)
    g3 = D("gpsimd", lambda e: e.dma_start(
        out=ksb[:], in_=kvout_d[:, 2048:2064]), inc="sKv", n=16)
    W("vector", "sKv", g3)
    kvc1 = D("vector", lambda e: e.tensor_copy(
        kvbf[:, :, 0:128], kvsb[:]), inc="vC")
    kvbf_c = D("vector", lambda e: e.tensor_copy(
        kvbf[:, :, 128:129], ksb[:].rearrange("p (a c) -> p a c", c=1)),
        inc="vC")

    emit_ws_chunk(32)
    W("vector", "aC", ws_abs[32])
    ws_sum = {}
    for w in range(3):
        t = D("vector", lambda e, w=w: e.reduce_sum(
            swcols[:, w:w + 1], wsabs[:, w * 11:(w + 1) * 11], axis=AX),
            inc="vC")
        W("vector", "vC", t)
        ws_sum[w] = t
    ws_done = ws_sum[2]
    W("gpsimd", "vC", ws_done)
    g1 = D("gpsimd", lambda e: e.dma_start(out=wsin_d, in_=swcols[:, 0:4]),
           inc="sM", n=16)
    W("gpsimd", "sM", g1)
    cc2_done = D("gpsimd", lambda e: e.collective_compute(
        "AllReduce", ALU.add, replica_groups=[list(range(8))],
        ins=[wsin_d.opt()], outs=[wsout_d.opt()]), inc="cc2")
    W("gpsimd", "cc2", cc2_done)
    ws_back = D("gpsimd", lambda e: e.dma_start(out=swcols[:, 0:4], in_=wsout_d),
                inc="sM", n=16)

    q_ev, wq_cast, qmm = {}, {}, {}
    for ot in range(NJ):
        wsemq = "sW" if ot % 2 == 0 else "sW2"
        if ot == 0:
            W("sync", "aC", p1_xn[NT - 1])
            W("sync", "vC", wv_cast)
        if ot >= 2:
            W("sync", "vC", wq_cast[ot - 2])
        for half in range(2):
            D("sync", lambda e, ot=ot, half=half: e.dma_start(
                out=wfst[:, ot % 2, half * 8:(half + 1) * 8, :],
                in_=wqt_d[:, ot * 128:(ot + 1) * 128].rearrange(
                    "(j p) o -> p j o", p=128)[:, half * 8:(half + 1) * 8, :]),
                inc=wsemq, n=16)
        W("vector", wsemq, P.c(wsemq))
        if ot >= 2:
            W("vector", "tC", qmm[ot - 2])
        wq_cast[ot] = D("vector", lambda e, ot=ot: e.tensor_copy(
            wqb[:, ot % 2], wfst[:, ot % 2]), inc="vC")
        for tc in range(4):
            gi = ot * 4 + tc
            b = gi % 8
            par = gi % 2
            if tc == 0:
                W("tensor", "vC", wq_cast[ot])
            if gi >= 8:
                W("tensor", "vC", q_ev[gi - 8])
            for j in range(NJ):
                D("tensor", lambda e, j=j, ot=ot, tc=tc, b=b: e.matmul(
                    pbank(b), wqb[:, ot % 2, j, :],
                    xnT[:, j, tc * 512:(tc + 1) * 512],
                    start=(j == 0), stop=(j == NJ - 1)),
                    **(dict(inc="tC") if j == NJ - 1 else {}))
            qmm[ot] = P.c("tC")
            W("vector", "tC", qmm[ot])
            e1 = D("vector", lambda e, b=b, par=par: e.tensor_scalar(
                elut[:, par], pbank(b), 0.0, None, ALU.min), inc="vC")
            W("scalar", "vC", e1)
            e2 = D("scalar", lambda e, par=par: e.activation(
                elue[:, par], elut[:, par], AF.Exp), inc="aC")
            W("vector", "aC", e2)
            q_ev[gi] = D("vector", lambda e, b=b, par=par, ot=ot, tc=tc:
                         e.scalar_tensor_tensor(
                             qT[:, ot, tc * 512:(tc + 1) * 512], pbank(b), 0.0,
                             elue[:, par], ALU.max, ALU.add), inc="vC")
    p3_last = q_ev[63]

    W("tensor", "vC", kvbf_c)
    W("tensor", "vC", p3_last)
    p4_z, p4_tev, p4_trmm = {}, {}, {}
    for tc in range(NT):
        par = tc % 2
        ob0 = 0 if par == 0 else 4
        dbk = 4 if par == 0 else 0
        if tc >= 2:
            W("tensor", "vC", p4_tev[tc - 2][-1])
        if tc >= 1:
            W("tensor", "vC", p4_tev[tc - 1][0])
            W("tensor", "vC", p4_z[tc - 1])
        for h in range(NH):
            D("tensor", lambda e, h=h, tc=tc, dbk=dbk: e.matmul(
                ps[:, dbk * 512 + h: dbk * 512 + h + 1],
                qT[:, h, tc * 128:(tc + 1) * 128], kvbf[:, h, 128:129],
                start=(h == 0), stop=(h == NH - 1)),
                **(dict(inc="tC") if h == NH - 1 else {}))
        den_mm = P.c("tC")
        for h in range(NH):
            b = ob0 + h // 4
            D("tensor", lambda e, h=h, tc=tc, b=b: e.matmul(
                ps[:, b * 512 + (h % 4) * 128: b * 512 + (h % 4) * 128 + 128],
                qT[:, h, tc * 128:(tc + 1) * 128], kvbf[:, h, 0:128],
                start=(h % 4 == 0), stop=(h % 4 == 3)),
                **(dict(inc="tC") if h == NH - 1 else {}))
        out_mm = P.c("tC")
        W("vector", "tC", den_mm)
        z1 = D("vector", lambda e, dbk=dbk, par=par: e.tensor_scalar(
            zsb[:, par], ps[:, dbk * 512: dbk * 512 + 16], 1e-6, None, ALU.add),
            inc="vC")
        W("vector", "vC", z1)
        p4_z[tc] = D("vector", lambda e, par=par: e.reciprocal(
            zsb[:, par], zsb[:, par]), inc="vC")
        W("vector", "tC", out_mm)
        W("vector", "vC", p4_z[tc])
        if tc == 0:
            W("vector", "tC", kvmm[(3, NT - 1)])
        if tc >= 2:
            W("vector", "tC", p4_trmm[tc - 2])
        W("scalar", "tC", out_mm)
        W("scalar", "vC", p4_z[tc])
        if tc == 0:
            W("scalar", "tC", kvmm[(3, NT - 1)])
        if tc >= 2:
            W("scalar", "tC", p4_trmm[tc - 2])
        for h in range(8):
            b = ob0 + h // 4
            kw = dict(inc="aC") if h == 7 else {}
            D("scalar", lambda e, h=h, b=b, par=par: e.activation(
                apre[:, par, h * 128:(h + 1) * 128],
                ps[:, b * 512 + (h % 4) * 128: b * 512 + (h % 4) * 128 + 128],
                AF.Identity, scale=zsb[:, par, h:h + 1]), **kw)
        aev_a = P.c("aC")
        for h in range(8, NH):
            b = ob0 + h // 4
            kw = dict(inc="vC") if h == NH - 1 else {}
            D("vector", lambda e, h=h, b=b, par=par: e.tensor_scalar(
                apre[:, par, h * 128:(h + 1) * 128],
                ps[:, b * 512 + (h % 4) * 128: b * 512 + (h % 4) * 128 + 128],
                zsb[:, par, h:h + 1], None, ALU.mult), **kw)
        aev = P.c("vC")
        W("tensor", "vC", aev)
        W("tensor", "aC", aev_a)
        evs = []
        for g in range(4):
            b = ob0 + g
            for jj in range(4):
                j = g * 4 + jj
                kw = dict(inc="tC") if jj == 3 else {}
                D("tensor", lambda e, j=j, b=b, jj=jj, par=par: e.matmul(
                    ps[:, b * 512 + jj * 128: b * 512 + jj * 128 + 128],
                    apre[:, par, j * 128:(j + 1) * 128], identb[:],
                    start=(jj == 0), stop=(jj == 3)), **kw)
            W("vector", "tC", P.c("tC"))
            evs.append(D("vector", lambda e, g=g, b=b, tc=tc: e.tensor_copy(
                ApreT[:, g * 4:(g + 1) * 4, tc * 128:(tc + 1) * 128],
                pbank(b).rearrange("p (a c) -> p a c", a=4)), inc="vC"))
        p4_trmm[tc] = P.c("tC")
        p4_tev[tc] = evs
    p4_last_ev = p4_tev[NT - 1][-1]
    p4_prev_ev = p4_tev[NT - 2][-1]
    p4_end_mm = P.c("tC")

    W("tensor", "vC", p4_last_ev)
    W("tensor", "vC", p4_prev_ev)
    W("gpsimd", "vC", wq_cast[NJ - 1])
    W("gpsimd", "aC", p1_xn[NT - 1])
    p1b_ka, p1b_k2, p1b_tr = {}, {}, {}
    trsb = {"gidx": 0, "ev": {}}

    def p1b_transpose_tile(src_ap, tcol):
        for g in range(4):
            gi = trsb["gidx"] + g
            b = gi % 4
            if gi - 4 in trsb["ev"]:
                W("tensor", "vC", trsb["ev"][gi - 4])
            for jj in range(4):
                j = g * 4 + jj
                kw = dict(inc="tC") if jj == 3 else {}
                D("tensor", lambda e, j=j, b=b, jj=jj: e.matmul(
                    ps[:, b * 512 + jj * 128: b * 512 + jj * 128 + 128],
                    src_ap[:, j * 128:(j + 1) * 128], identb[:],
                    start=(jj == 0), stop=(jj == 3)), **kw)
            W("vector", "tC", P.c("tC"))
            trsb["ev"][gi] = D("vector", lambda e, g=g, b=b, tcol=tcol:
                               e.tensor_copy(
                                   KTs[:, g * 4:(g + 1) * 4, tcol:tcol + 128],
                                   pbank(b).rearrange("p (a c) -> p a c", a=4)),
                               inc="vC")
        trsb["gidx"] += 4
        return P.c("tC"), trsb["ev"][trsb["gidx"] - 1]

    def p1b_tile(i):
        if i >= 2:
            W("gpsimd", "aC", p1b_ka[i - 2])
        d = D("gpsimd", lambda e, i=i: e.dma_start(
            out=xtile[:, i % 2], in_=x_d[i * 128:(i + 1) * 128, :]),
            inc="sXa" if i % 2 == 0 else "sXb", n=16)
        W("scalar", "sXa" if i % 2 == 0 else "sXb", d)
        if i == 0:
            W("scalar", "aC", ws_abs[32])
        if i >= 1:
            W("scalar", "vC", p1b_k2[i - 1])
        p1b_ka[i] = D("scalar", lambda e, i=i: e.activation(
            kpre[:], xtile[:, i % 2], AF.Identity,
            bias=st(S_NMRSA, i), scale=st(S_RSA, i)), inc="aC")
        W("vector", "aC", p1b_ka[i])
        if i == 0:
            W("vector", "tC", p4_end_mm)
        if i >= 2:
            W("vector", "tC", p1b_tr[i - 2][0])
        k1 = D("vector", lambda e, i=i: e.tensor_scalar(
            kpre[:], kpre[:], MAGIC, MAGIC, ALU.add, ALU.subtract), inc="vC")
        W("vector", "vC", k1)
        p1b_k2[i] = D("vector", lambda e, i=i: e.tensor_scalar(
            ktok[:, i % 2], kpre[:], st(S_DQA, i), None, ALU.mult), inc="vC")
        W("tensor", "vC", p1b_k2[i])
        p1b_tr[i] = p1b_transpose_tile(ktok[:, i % 2], i * 128)

    wo_cast, p5_ev, p5_st, p5_mm, p5_q = {}, {}, {}, {}, {}
    NOC = 4

    def p5_stage(oc):
        for q in range(8):
            gq = oc * 8 + q
            wsemo = "sW" if gq % 2 == 0 else "sW2"
            if gq == 0:
                W("sync", "vC", wq_cast[NJ - 1])
                W("sync", "vC", wq_cast[NJ - 2])
            if gq >= 2:
                W("sync", "aC", p5_q[gq - 2])
            D("sync", lambda e, oc=oc, q=q: e.dma_start(
                out=wostg[:, q % 2],
                in_=wot_d[:, oc * 512:(oc + 1) * 512].rearrange(
                    "(j p) o -> p j o", p=128)[:, q * 2:(q + 1) * 2, :]),
                inc=wsemo, n=16)
            dW = P.c(wsemo)
            W("scalar", wsemo, dW)
            if gq == 0:
                W("scalar", "tC", qmm[NJ - 1])
                W("scalar", "vC", p3_last)
            if q == 0 and oc >= 1:
                W("scalar", "tC", p5_mm[oc - 1])
            p5_q[gq] = D("scalar", lambda e, q=q: e.activation(
                wob[:, q * 2:(q + 1) * 2, :].rearrange("p a b -> p (a b)"),
                wostg[:, q % 2].rearrange("p a b -> p (a b)"), AF.Identity),
                inc="aC")
        wo_cast[oc] = p5_q[oc * 8 + 7]

    def p5_group(oc, tc):
        gi = oc * NT + tc
        b = 4 + gi % 4
        if tc == 0:
            W("tensor", "aC", wo_cast[oc])
        if gi >= 4:
            W("tensor", "aC", p5_ev[gi - 4])
        for j in range(NJ):
            D("tensor", lambda e, j=j, tc=tc, b=b: e.matmul(
                pbank(b), ApreT[:, j, tc * 128:(tc + 1) * 128],
                wob[:, j, :], start=(j == 0), stop=(j == NJ - 1)),
                **(dict(inc="tC") if j == NJ - 1 else {}))
        p5_mm[oc] = P.c("tC")
        W("scalar", "tC", p5_mm[oc])
        if gi >= 2:
            W("scalar", "sYa" if gi % 2 == 0 else "sYb", p5_st[gi - 2])
        p5_ev[gi] = D("scalar", lambda e, b=b, par=gi % 2: e.activation(
            y2pc[:, par], pbank(b), AF.Identity), inc="aC")
        W("gpsimd", "aC", p5_ev[gi])
        p5_st[gi] = D("gpsimd", lambda e, tc=tc, oc=oc, par=gi % 2:
                      e.dma_start(
                          out=y2_d[tc * 128:(tc + 1) * 128,
                                   oc * 512:(oc + 1) * 512],
                          in_=y2pc[:, par]), inc="sYa" if gi % 2 == 0
                      else "sYb", n=16)

    for u in range(16):
        if u % 4 == 0:
            p5_stage(u // 4)
        p1b_tile(u)
        for tcq in range(4):
            p5_group(u // 4, (u % 4) * 4 + tcq)
    p1b_last = p1b_tr[NT - 1][1]
    p5_final_mm = P.c("tC")
    p5_final_aev = P.c("aC")

    W("tensor", "sM", ws_back)
    W("tensor", "aC", p5_final_aev)
    D("tensor", lambda e: e.matmul(
        ps[0:1, 7 * 512: 7 * 512 + 4], onesf_c[:], swcols[:, 0:4],
        start=True, stop=True), inc="tC")
    t_ws = P.c("tC")
    W("scalar", "tC", t_ws)
    a_ws = D("scalar", lambda e: e.activation(
        swrow[:, 0:4], ps[0:1, 7 * 512: 7 * 512 + 4], AF.Identity), inc="aC")
    W("vector", "aC", a_ws)
    r1 = D("vector", lambda e: e.tensor_scalar(
        swrow[:, 4:8], swrow[:, 0:4], 1.0 / (HID * CDIM), 1e-5,
        ALU.mult, ALU.max), inc="vC")
    W("vector", "vC", r1)
    r2 = D("vector", lambda e: e.reciprocal(swrow[:, 8:12], swrow[:, 4:8]),
           inc="vC")
    W("vector", "vC", r2)
    W("tensor", "vC", r2)
    for w in range(3):
        D("tensor", lambda e, w=w: e.matmul(
            ps[:, 7 * 512 + 16 + w: 7 * 512 + 16 + w + 1],
            onesf_r[:], swrow[:, 8 + w: 8 + w + 1], start=(w == 0),
            stop=False))
        D("tensor", lambda e, w=w: e.matmul(
            ps[:, 7 * 512 + 24 + w: 7 * 512 + 24 + w + 1],
            onesf_r[:], swrow[:, 4 + w: 4 + w + 1], start=False,
            stop=(w == 2)),
            **(dict(inc="tC") if w == 2 else {}))
    t_bc = P.c("tC")
    W("vector", "tC", t_bc)
    sw_c1 = D("vector", lambda e: e.tensor_copy(
        swcols[:, 4:7], ps[:, 7 * 512 + 16: 7 * 512 + 19]), inc="vC")
    sw_c2 = D("vector", lambda e: e.tensor_copy(
        swcols[:, 9:12], ps[:, 7 * 512 + 24: 7 * 512 + 27]), inc="vC")

    W("vector", "tC", p5_final_mm)
    mx_init = D("vector", lambda e: e.memset(mxacc[:], 0.0), inc="vC")

    fa_t1, fa_t2, fa_hd, fa_mm, hst, fa_q2 = {}, {}, {}, {}, {}, {}
    fa_q1 = {}

    def emit_fa_quant(m):
        wsem = "sW" if m % 2 == 0 else "sW2"
        if m == 0:
            W("sync", "aC", p1b_ka[NT - 1])
            W("sync", "aC", p5_q[NOC * 8 - 1])
        if m >= 2:
            W("sync", "vC", fa_q1[m - 2])
        for half in range(2):
            D("sync", lambda e, m=m, half=half: e.dma_start(
                out=wfst[:, m % 2, half * 8:(half + 1) * 8, :],
                in_=w1t_d[:, m * 128:(m + 1) * 128].rearrange(
                    "(j p) h -> p j h", p=128)[:, half * 8:(half + 1) * 8, :]),
                inc=wsem, n=16)
        d1 = P.c(wsem)
        W("vector", wsem, d1)
        if m == 0:
            W("vector", "vC", sw_c2)
            W("vector", "tC", p1b_tr[NT - 1][0])
        if m >= 2:
            W("vector", "tC", fa_mm[m - 2])
        wa = wfst[:, m % 2].rearrange("p a b -> p (a b)")
        q1 = D("vector", lambda e, wa=wa: e.tensor_scalar(
            qtmp[:], wa, swcols[:, 4:5], MAGIC, ALU.mult, ALU.add), inc="vC")
        fa_q1[m] = q1
        W("vector", "vC", q1)
        c1v = D("vector", lambda e: e.tensor_scalar(
            qtmp[:], qtmp[:], MAGIC - 1.0, MAGIC + 1.0, ALU.max, ALU.min),
            inc="vC")
        W("vector", "vC", c1v)
        fa_t1[m] = D("vector", lambda e, m=m: e.tensor_scalar(
            t1b[:, m % 2].rearrange("p a b -> p (a b)"), qtmp[:],
            MAGIC, None, ALU.subtract), inc="vC")
        if m == 0:
            W("sync", "tC", p5_final_mm)
        wsem2c = "sF2a" if m % 2 == 0 else "sF2b"
        if m >= 2:
            W("sync", "vC", fa_q2[m - 2])
        for half in range(2):
            D("sync", lambda e, m=m, half=half: e.dma_start(
                out=wfst2[:, m % 2, half * 8:(half + 1) * 8, :],
                in_=w2t_d[:, m * 128:(m + 1) * 128].rearrange(
                    "(j p) h -> p j h", p=128)[:, half * 8:(half + 1) * 8, :]),
                inc=wsem2c, n=16)
        d2 = P.c(wsem2c)
        W("vector", wsem2c, d2)
        wa2 = wfst2[:, m % 2].rearrange("p a b -> p (a b)")
        if m == 0:
            W("vector", "tC", p5_final_mm)
        q2 = D("vector", lambda e, wa2=wa2: e.tensor_scalar(
            qtm2[:], wa2, swcols[:, 5:6], MAGIC, ALU.mult, ALU.add), inc="vC")
        fa_q2[m] = q2
        W("vector", "vC", q2)
        c2v = D("vector", lambda e: e.tensor_scalar(
            qtm2[:], qtm2[:], MAGIC - 1.0, MAGIC + 1.0, ALU.max, ALU.min),
            inc="vC")
        W("vector", "vC", c2v)
        fa_t2[m] = D("vector", lambda e, m=m: e.tensor_scalar(
            t2b[:, m % 2].rearrange("p a b -> p (a b)"), qtm2[:],
            MAGIC, None, ALU.subtract), inc="vC")

    def emit_fa_pe(m):
        for tc in range(4):
            it = m * 4 + tc
            by1, by2 = it % 4, 4 + it % 4
            if tc == 0:
                W("tensor", "vC", fa_t2[m])
            if it == 0:
                W("tensor", "vC", mx_init)
            if it >= 4:
                W("tensor", "aC", fa_hd[it - 4][0])
                W("tensor", "vC", fa_hd[it - 4][1])
            for j in range(NJ):
                D("tensor", lambda e, j=j, m=m, tc=tc, by1=by1: e.matmul(
                    pbank(by1), t1b[:, m % 2, j, :],
                    KTs[:, j, tc * 512:(tc + 1) * 512],
                    start=(j == 0), stop=(j == NJ - 1)),
                    **(dict(inc="tC") if j == NJ - 1 else {}))
            mm1 = P.c("tC")
            for j in range(NJ):
                D("tensor", lambda e, j=j, m=m, tc=tc, by2=by2: e.matmul(
                    pbank(by2), t2b[:, m % 2, j, :],
                    KTs[:, j, tc * 512:(tc + 1) * 512],
                    start=(j == 0), stop=(j == NJ - 1)),
                    **(dict(inc="tC") if j == NJ - 1 else {}))
            mm2 = P.c("tC")
            fa_mm[m] = mm2
            W("scalar", "tC", mm1)
            if it >= 1:
                W("scalar", "vC", fa_hd[it - 1][1])
            s1 = D("scalar", lambda e, by1=by1: e.activation(
                h1t[:], pbank(by1), AF.Silu, scale=swcols[:, 9:10]), inc="aC")
            W("scalar", "tC", mm2)
            s2 = D("scalar", lambda e, by2=by2: e.activation(
                h2t[:], pbank(by2), AF.Identity, scale=swcols[:, 10:11]),
                inc="aC")
            W("vector", "aC", s2)
            if m >= 2:
                W("vector", "sHa" if m % 2 == 0 else "sHb", hst[m - 2])
            if it == 0:
                W("vector", "sYa", p5_st[62])
                W("vector", "sYb", p5_st[63])
            hv = D("vector", lambda e, m=m, tc=tc: e.tensor_tensor(
                hrow[:, m % 2, tc * 512:(tc + 1) * 512], h1t[:], h2t[:],
                ALU.mult), inc="vC")
            W("scalar", "vC", hv)
            habs = D("scalar", lambda e, m=m, tc=tc: e.activation(
                h1t[:], hrow[:, m % 2, tc * 512:(tc + 1) * 512], AF.Abs),
                inc="aC")
            W("vector", "aC", habs)
            mxv = D("vector", lambda e, tc=tc: e.tensor_tensor(
                mxacc[:, tc * 512:(tc + 1) * 512],
                mxacc[:, tc * 512:(tc + 1) * 512], h1t[:], ALU.max),
                inc="vC")
            fa_hd[it] = (s2, mxv, hv)
        W("gpsimd", "vC", fa_hd[m * 4 + 3][1])
        hst[m] = D("gpsimd", lambda e, m=m: e.dma_start(
            out=h_d[m * 128:(m + 1) * 128, :], in_=hrow[:, m % 2]),
            inc="sHa" if m % 2 == 0 else "sHb", n=16)

    emit_fa_quant(0)
    for m in range(44):
        if m + 1 < 44:
            emit_fa_quant(m + 1)
        emit_fa_pe(m)
    fa_last_mm = fa_mm[43]
    fa_last_aev = P.c("aC")

    W("tensor", "vC", fa_hd[44 * 4 - 1][1])
    W("tensor", "aC", fa_last_aev)
    fold_ev = {}
    for bi in range(16):
        b = bi % 4
        if bi >= 4:
            W("tensor", "vC", fold_ev[bi - 4])
        D("tensor", lambda e, bi=bi, b=b: e.matmul(
            ps[:, b * 512: b * 512 + 128],
            mxacc[:, bi * 128:(bi + 1) * 128], identf[:],
            start=True, stop=True), inc="tC")
        W("vector", "tC", P.c("tC"))
        fold_ev[bi] = D("vector", lambda e, bi=bi, b=b: e.reduce_max(
            st(S_MXH, bi), ps[:, b * 512: b * 512 + 128], axis=AX), inc="vC")
    W("vector", "vC", fold_ev[15])
    c1_ = D("vector", lambda e: e.tensor_scalar(
        st(S_MXH), st(S_MXH), 1e-5, None, ALU.max), inc="vC")
    W("vector", "vC", c1_)
    d3 = D("vector", lambda e: e.tensor_scalar(
        st(S_DQ3), st(S_MXH), 1.0 / 127.0, swcols[:, 11:12],
        ALU.mult, ALU.mult), inc="vC")
    shc = D("vector", lambda e: e.reciprocal(st(S_SH), st(S_MXH)), inc="vC")
    W("vector", "vC", shc)
    shc2 = D("vector", lambda e: e.tensor_scalar(
        st(S_SH), st(S_SH), 127.0, None, ALU.mult), inc="vC")
    W("tensor", "vC", shc2)
    D("tensor", lambda e: e.matmul(
        ps[0:16, 4 * 512: 4 * 512 + 128], st(S_SH), identf[:],
        start=True, stop=True), inc="tC")
    t_sh = P.c("tC")
    W("scalar", "tC", t_sh)
    a_shr = D("scalar", lambda e: e.activation(
        rows[:], ps[0:16, 4 * 512: 4 * 512 + 128], AF.Identity), inc="aC")
    W("gpsimd", "aC", a_shr)
    W("gpsimd", "tC", p1b_tr[NT - 1][0])
    W("gpsimd", "vC", d3)
    g_shr = D("gpsimd", lambda e: e.dma_start(
        out=shr[:], in_=rows[:]), inc="sM", n=16)
    W("tensor", "sM", g_shr)
    bcsh_ev = {}
    for q in range(4):
        bq = 5 + q % 2
        if q >= 2:
            W("tensor", "vC", bcsh_ev[q - 2])
        D("tensor", lambda e, q=q, bq=bq: e.matmul(
            pbank(bq), onesf_r[:], shr[:, q * 512:(q + 1) * 512],
            start=True, stop=True), inc="tC")
        W("vector", "tC", P.c("tC"))
        bcsh_ev[q] = D("vector", lambda e, q=q, bq=bq: e.tensor_copy(
            bcsh[:, q * 512:(q + 1) * 512], pbank(bq)), inc="vC")
    bcsh_done = bcsh_ev[3]

    NMAC_N = 11
    out_st = {}
    fstate = {"f1_prev": None, "fin2": {}, "o3r": {}}

    def emit_final_tile(g2f, tc):
        ti = g2f * 8 + tc
        lg = (g2f + 1) * NMAC_N - 1
        if g2f == 0 and tc == 0:
            W("gpsimd", "aC", wo_cast[NOC - 1])
            W("gpsimd", "sYa", p5_st[62])
            W("gpsimd", "sYb", p5_st[63])
        if (g2f, tc) != (0, 0):
            pst = out_st[(g2f, tc - 1)] if tc >= 1 else out_st[(0, 7)]
            W("gpsimd", "sOa" if (tc - 1) % 2 == 0 else "sOb", pst)
        dx = D("gpsimd", lambda e, ti=ti: e.dma_start(
            out=outb[:], in_=x_d[ti * 128:(ti + 1) * 128, :]),
            inc="sXa", n=16)
        dy = D("gpsimd", lambda e, ti=ti, tc=tc: e.dma_start(
            out=y2in[:, tc % 2], in_=y2_d[ti * 128:(ti + 1) * 128, :]),
            inc="sYl", n=16)
        for _p in (tc * 2, tc * 2 + 1):
            sm, vv = fb_ev[(lg, _p)]
            W("vector", sm, vv)
        W("vector", "sXa", dx)
        f1 = D("vector", lambda e, tc=tc, ti=ti: e.scalar_tensor_tensor(
            outb[:], out3[:, tc, :], st(S_DQ3, ti),
            outb[:], ALU.mult, ALU.add), inc="vC")
        fstate.setdefault("f1", {})[(g2f, tc)] = f1
        W("gpsimd", "vC", f1)
        W("gpsimd", "sYl", dy)
        fin2 = D("gpsimd", lambda e, tc=tc: e.tensor_tensor(
            outb[:], outb[:], y2in[:, tc % 2], ALU.add), inc="pC")
        W("gpsimd", "pC", fin2)
        out_st[(g2f, tc)] = D("gpsimd", lambda e, ti=ti: e.dma_start(
            out=out_d[ti * 128:(ti + 1) * 128, :], in_=outb[:]),
            inc="sOa" if tc % 2 == 0 else "sOb", n=16)

    prev_mm_by_par = {0: P.c("tC"), 1: P.c("tC")}
    prev_kh_by_par = {0: None, 1: None}
    q3_par = {}
    qbox = {"prev_qh": None, "first_dh": True}
    fb_ev = {}
    fb_kh_all, fb_t3_all = {}, {}
    NSUB = 4

    def emit_quant(gmi):
        g2q = gmi // NMAC_N
        miq = gmi % NMAC_N
        tcoq = g2q * 1024
        par = gmi % 2
        h0 = miq * 512
        for s in range(NSUB):
            wsem3 = ("sW", "sW2", "sF2a", "sF2b")[s]
            if q3_par.get(s) is not None:
                W("sync", "aC", q3_par[s])
            if gmi == 0 and s == 0:
                W("sync", "vC", fa_t2[43])
            if gmi == 0 and s == 3:
                W("sync", "vC", bcsh_done)
            d = D("sync", lambda e, h0=h0, s=s: e.dma_start(
                out=wf3[:, s],
                in_=w3t_d[h0 + s * 128: h0 + (s + 1) * 128, :]),
                inc=wsem3, n=16)
            W("scalar", wsem3, d)
            if gmi == 0 and s == 0:
                W("scalar", "tC", fa_last_mm)
            if qbox.get("t3p", {}).get(s % 2) is not None:
                W("scalar", "vC", qbox["t3p"][s % 2])
            q3 = D("scalar", lambda e, s=s: e.activation(
                qt2[:, s % 2], wf3[:, s], AF.Identity,
                scale=swcols[:, 6:7], bias=pmag[:]), inc="aC")
            q3_par[s] = q3
            W("vector", "aC", q3)
            if s == 0:
                W("vector", "tC", prev_mm_by_par[par])
            c3v = D("vector", lambda e, s=s: e.tensor_scalar(
                qt2[:, s % 2], qt2[:, s % 2], MAGIC - 1.0, MAGIC + 1.0,
                ALU.max, ALU.min), inc="vC")
            W("vector", "vC", c3v)
            fb_t3_all[(gmi, s)] = ("vC", D(
                "vector", lambda e, s=s, par=par: e.tensor_scalar(
                    t3b[:, par, s, :], qt2[:, s % 2], MAGIC, None,
                    ALU.subtract), inc="vC"))
            qbox.setdefault("t3p", {})[s % 2] = fb_t3_all[(gmi, s)][1]
            k = gmi * NSUB + s
            if qbox["first_dh"]:
                W("gpsimd", "tC", fa_last_mm)
                W("gpsimd", "sHa", hst[42])
                W("gpsimd", "sHb", hst[43])
                qbox["first_dh"] = False
            if qbox.get(("qh", k - 4)) is not None:
                W("gpsimd", "pC", qbox[("qh", k - 4)])
            hsem = ("sHl", "sHm", "sHa", "sHb")[k % 4]
            dh = D("gpsimd", lambda e, h0=h0, s=s, tcoq=tcoq, k=k:
                   e.dma_start(
                       out=hinb[:, k % 4, :],
                       in_=h_d[h0 + s * 128: h0 + (s + 1) * 128,
                               tcoq:tcoq + 1024]), inc=hsem, n=16)
            W("gpsimd", hsem, dh)
            W("gpsimd", "vC", bcsh_done)
            if qbox.get(("kh", k - 2)) is not None:
                W("gpsimd", "vC", qbox[("kh", k - 2)])
            qh = D("gpsimd", lambda e, tcoq=tcoq, k=k:
                   e.tensor_tensor(
                       qh2b[:, k % 2], hinb[:, k % 4, :],
                       bcsh[:, tcoq:tcoq + 1024], ALU.mult), inc="pC")
            qbox[("qh", k)] = qh
            W("vector", "pC", qh)
            fb_kh_all[(gmi, s)] = D(
                "vector", lambda e, s=s, par=par, k=k: e.tensor_scalar(
                    khb[:, par, s, :], qh2b[:, k % 2], MAGIC, MAGIC,
                    ALU.add, ALU.subtract), inc="vC")
            qbox[("kh", k)] = fb_kh_all[(gmi, s)]

    def emit_pe(gmi):
        miq = gmi % NMAC_N
        par = gmi % 2
        for pos in range(16):
            tc, ch = pos // 2, pos % 2
            pb = (pos % 4) * 2
            if pos == 0:
                W("tensor", "vC", fb_kh_all[(gmi, NSUB - 1)])
                W("tensor", fb_t3_all[(gmi, NSUB - 1)][0],
                  fb_t3_all[(gmi, NSUB - 1)][1])
            if pos >= 4:
                sm, vv = fb_ev[(gmi, pos - 4)]
                W("tensor", sm, vv)
            elif gmi >= 1:
                sm, vv = fb_ev[(gmi - 1, 12 + pos)]
                W("tensor", sm, vv)
            else:
                if pos == 0:
                    W("tensor", "vC", fold_ev[15])
                    W("tensor", "vC", bcsh_done)
                    W("tensor", "aC", a_shr)
            for s in range(NSUB):
                for oq in range(2):
                    kw = (dict(inc="tC")
                          if (s == NSUB - 1 and oq == 1) else {})
                    D("tensor", lambda e, s=s, oq=oq, tc=tc, ch=ch, pb=pb,
                      par=par:
                      e.matmul(
                          pbank(pb + oq),
                          khb[:, par, s, tc * 128:(tc + 1) * 128],
                          t3b[:, par, s,
                              ch * 1024 + oq * 512:ch * 1024 + oq * 512
                              + 512],
                          start=(s == 0), stop=(s == NSUB - 1)), **kw)
            fb_mm = P.c("tC")
            if pos == 15:
                prev_mm_by_par[par] = fb_mm
            if gmi == NMAC_N and ch == 0:
                emit_final_tile(0, tc)
            if miq == 0:
                W("scalar", "tC", fb_mm)
                if gmi == NMAC_N:
                    W("scalar", "vC", fstate["f1"][(0, tc)])
                v = D("scalar", lambda e, tc=tc, ch=ch, pb=pb: e.activation(
                    out3[:, tc, ch * 1024:(ch + 1) * 1024], pbank(pb, 2),
                    AF.Identity), inc="aC")
                fb_ev[(gmi, pos)] = ("aC", v)
            else:
                W("vector", "tC", fb_mm)
                v = D("vector", lambda e, tc=tc, ch=ch, pb=pb:
                      e.scalar_tensor_tensor(
                          out3[:, tc, ch * 1024:(ch + 1) * 1024],
                          pbank(pb, 2), 1.0,
                          out3[:, tc, ch * 1024:(ch + 1) * 1024],
                          ALU.mult, ALU.add), inc="vC")
                fb_ev[(gmi, pos)] = ("vC", v)

    emit_quant(0)
    for gmi in range(2 * NMAC_N):
        if gmi + 1 < 2 * NMAC_N:
            emit_quant(gmi + 1)
        emit_pe(gmi)
    for tc in range(8):
        emit_final_tile(1, tc)
    W("gpsimd", "sOa", P.c("sOa"))
    W("gpsimd", "sOb", P.c("sOb"))

    with nc.Block() as block:
        @block.sync
        def _(sync):
            for fn in P.q["sync"]:
                fn(sync)

        @block.vector
        def _(vector):
            for fn in P.q["vector"]:
                fn(vector)

        @block.scalar
        def _(scalar):
            for fn in P.q["scalar"]:
                fn(scalar)

        @block.tensor
        def _(tensor):
            for fn in P.q["tensor"]:
                fn(tensor)

        @block.gpsimd
        def _(gpsimd):
            for fn in P.q["gpsimd"]:
                fn(gpsimd)

    return nc


_NC_CACHE = {}


def kernel(**inputs):
    x = np.asarray(inputs["L0_attn_x"], dtype=np.float32)
    Wq = np.asarray(inputs["L0_attn_Wq"], dtype=np.float32)
    Wk = np.asarray(inputs["L0_attn_Wk"], dtype=np.float32)
    Wv = np.asarray(inputs["L0_attn_Wv"], dtype=np.float32)
    Wo = np.asarray(inputs["L0_attn_Wo"], dtype=np.float32)
    w1 = np.asarray(inputs["L0_ffn_w1"], dtype=np.float32)
    w2 = np.asarray(inputs["L0_ffn_w2"], dtype=np.float32)
    w3 = np.asarray(inputs["L0_ffn_w3"], dtype=np.float32)

    wqt = np.ascontiguousarray(Wq.T)
    wkt = np.ascontiguousarray(Wk.T)
    wvt = np.ascontiguousarray(Wv.T)
    wot = np.ascontiguousarray(Wo.T)
    w1t = np.ascontiguousarray(w1.T)
    w2t = np.ascontiguousarray(w2.T)
    w3t = np.ascontiguousarray(w3.T)
    idf = np.eye(128, dtype=np.float32)

    nsl = HID * CDIM // 8
    in_maps = []
    for c in range(8):
        b, g = c // 2, c % 2
        xs = np.ascontiguousarray(x[b, g * TOK:(g + 1) * TOK, :])
        wsc = np.concatenate(
            [w.reshape(-1)[c * nsl:(c + 1) * nsl].reshape(128, nsl // 128)
             for w in (w1, w2, w3)], axis=1)
        in_maps.append({"x": xs, "wqt": wqt, "wkt": wkt, "wvt": wvt,
                        "wot": wot, "w1t": w1t, "w2t": w2t, "w3t": w3t,
                        "wsc": np.ascontiguousarray(wsc), "idf": idf})

    if "nc" not in _NC_CACHE:
        _NC_CACHE["nc"] = build_nc()
    res = run_bass_kernel_spmd(_NC_CACHE["nc"], in_maps, core_ids=list(range(8)))
    out = np.zeros((4, 4096, CDIM), dtype=np.float32)
    for c in range(8):
        b, g = c // 2, c % 2
        out[b, g * TOK:(g + 1) * TOK, :] = res.results[c]["out"]
    return out
